# version 01 (final)
# baseline (speedup 1.0000x reference)
import contextlib
import numpy as np
import ml_dtypes

import concourse.bass as bass
import concourse.mybir as mybir
from concourse.bass_utils import run_bass_kernel_spmd

F32 = mybir.dt.float32
BF16 = mybir.dt.bfloat16
AF = mybir.ActivationFunctionType
ALU = mybir.AluOpType
AX = mybir.AxisListType

S = 4096
D = 2048
TT = 32
DC = 16
NEG = -1e30
SHIFT = 20.0
LN_EPS = 1e-5
GN_EPS = 1e-5

AH, AHD = 16, 128
ASCALE = 1.0 / np.sqrt(AHD)
NE = 16
NPASS = 10
IB = 11
I_MOE = 1408
I_SHH = 2816
SH, SKVH, SP = 32, 8, 64
CONV = 3072
PROJ = 5152
NCH = 32

RG_PAIRS = [[0, 1], [2, 3], [4, 5], [6, 7]]


ENGS = ("sync", "tensor", "vector", "scalar", "gpsimd")


class _Op:
    __slots__ = ("eng", "meth", "args", "kw", "incs")

    def __init__(self, eng, meth, args, kw):
        self.eng, self.meth, self.args, self.kw = eng, meth, args, kw
        self.incs = []


class _OpRef:
    __slots__ = ("rec", "op")

    def __init__(self, rec, op):
        self.rec, self.op = rec, op

    def inc(self, sem, n=1):
        self.op.incs.append((sem, n))
        self.rec.n[sem] += n
        return self


class _EngRec:
    def __init__(self, rec, eng):
        self._rec, self._eng = rec, eng

    def __getattr__(self, meth):
        def call(*args, **kw):
            op = _Op(self._eng, meth, args, kw)
            self._rec.ops.append(op)
            return _OpRef(self._rec, op)
        return call

    def wait(self, sem, val=None):
        if val is None:
            val = self._rec.n[sem]
        return self.__getattr__("wait_ge")(sem, val)

    def dma(self, out, in_, sem, **kw):
        return self.__getattr__("dma_start")(out, in_, **kw).inc(sem, 16)


class Rec:
    def __init__(self, semnames):
        self.ops = []
        self.n = {s: 0 for s in semnames}
        self.t = _EngRec(self, "tensor")
        self.v = _EngRec(self, "vector")
        self.s = _EngRec(self, "scalar")
        self.g = _EngRec(self, "gpsimd")
        self.y = _EngRec(self, "sync")

    def emit(self, block, sems):
        def mk(ename):
            def f(e):
                for op in self.ops:
                    if op.eng != ename:
                        continue
                    args = [sems[a] if isinstance(a, _SemName) else a
                            for a in op.args]
                    ins = getattr(e, op.meth)(*args, **op.kw)
                    for sem, k in op.incs:
                        ins.then_inc(sems[sem], k)
            return f
        block.sync(mk("sync"))
        block.tensor(mk("tensor"))
        block.vector(mk("vector"))
        block.scalar(mk("scalar"))
        block.gpsimd(mk("gpsimd"))


class _SemName(str):
    pass


SEMS = [_SemName(x) for x in
        ("LW", "LA", "LB", "ST", "PE", "VE", "SC", "GP", "CC", "AD")]
(SEM_LW, SEM_LA, SEM_LB, SEM_ST, SEM_PE, SEM_VE, SEM_SC, SEM_GP, SEM_CC,
 SEM_AD) = SEMS


def _rope_tables():
    freqs = 1.0 / 10000.0 ** (np.arange(0, AHD, 2, dtype=np.float32) / AHD)
    pos = np.arange(S, dtype=np.float32)
    ang = np.outer(pos, freqs)
    return np.cos(ang).astype(np.float32), np.sin(ang).astype(np.float32)


def _rope_perm():
    p = np.empty(D, dtype=np.int64)
    for h in range(AH):
        base = h * AHD
        p[base:base + 64] = base + 2 * np.arange(64)
        p[base + 64:base + 128] = base + 2 * np.arange(64) + 1
    return p


def _tile_w(w):
    return np.ascontiguousarray(w.reshape(DC, 128, -1))


def _consts():
    c = {}
    c["identb"] = np.eye(128, dtype=ml_dtypes.bfloat16)
    c["identf"] = np.eye(128, dtype=np.float32)
    c["onesb"] = np.ones((1, 128), dtype=ml_dtypes.bfloat16)
    c["onesf"] = np.ones((1, 128), dtype=np.float32)
    cos, sin = _rope_tables()
    c["ropec"], c["ropes"] = cos, sin
    m = np.zeros((4, 128, 512), dtype=np.float32)
    for j in range(4):
        for qs in range(4):
            q_global = qs * 128 + np.arange(128)[None, :]
            k_global = j * 128 + np.arange(128)[:, None]
            m[j][:, qs * 128:(qs + 1) * 128] = np.where(
                k_global > q_global, NEG, 0.0)
    c["maskT"] = m.reshape(4, 128, 512)
    sidx = np.arange(128)[:, None]
    lidx = np.arange(128)[None, :]
    c["segneg"] = np.where(lidx >= sidx, 0.0, -1e38).astype(np.float32)
    sm = np.ones(S, dtype=np.float32)
    sm[0::128] = 0.0
    c["scanmask"] = np.broadcast_to(sm, (SH, S)).copy()
    return c


def make_in_maps(inp):
    perm = _rope_perm()
    cst = _consts()
    wq = np.ascontiguousarray(_tile_w(inp["L0_attn_a_Wq"])[:, :, perm])
    wk = np.ascontiguousarray(_tile_w(inp["L0_attn_a_Wk"])[:, :, perm])
    wv = _tile_w(inp["L0_attn_a_Wv"])
    wo = _tile_w(inp["L0_attn_a_Wo"])
    wproj = _tile_w(np.ascontiguousarray(inp["L0_attn_b_W_vkqgdt"].T))
    wo_ssd = _tile_w(np.ascontiguousarray(inp["L0_attn_b_W_o"].T))
    convp = np.ascontiguousarray(
        inp["L0_attn_b_conv_w"].reshape(24, 128, 2).transpose(1, 0, 2).reshape(128, 48))
    convb = np.ascontiguousarray(
        inp["L0_attn_b_conv_b"].reshape(24, 128).T)
    dtb = inp["L0_attn_b_dt_bias"].reshape(SH, 1)
    alb = inp["L0_attn_b_A_log_bias"].reshape(SH, 1)
    gnw = inp["L0_attn_b_gnorm_w"].reshape(1, SP)
    rw = inp["L0_ffn_router_w"]
    shgw = inp["L0_ffn_sh_gate_w"]
    wg_all, wu_all, wd_all = (inp["L0_ffn_w_gate"], inp["L0_ffn_w_up"],
                              inp["L0_ffn_w_down"])
    shg, shu, shd = (inp["L0_ffn_sh_gate"], inp["L0_ffn_sh_up"],
                     inp["L0_ffn_sh_down"])
    x = inp["L0_attn_a_x"]

    maps = []
    for c in range(8):
        b, s = c // 2, c % 2
        own = list(range(s * 8, s * 8 + 8))
        other = list(range((1 - s) * 8, (1 - s) * 8 + 8))
        rt17 = np.concatenate(
            [rw[own].T, rw[other].T, shgw], axis=1)
        wg = np.empty((NPASS, IB, DC, 128, 128), dtype=np.float32)
        wu = np.empty_like(wg)
        wd = np.empty((NPASS, IB, 128, D), dtype=np.float32)
        for p in range(8):
            e = own[p]
            wg[p] = wg_all[e].reshape(DC, 128, IB, 128).transpose(2, 0, 1, 3)
            wu[p] = wu_all[e].reshape(DC, 128, IB, 128).transpose(2, 0, 1, 3)
            wd[p] = wd_all[e].reshape(IB, 128, D)
        for p in (8, 9):
            lo = s * I_SHH + (p - 8) * I_MOE
            wg[p] = shg[:, lo:lo + I_MOE].reshape(DC, 128, IB, 128).transpose(2, 0, 1, 3)
            wu[p] = shu[:, lo:lo + I_MOE].reshape(DC, 128, IB, 128).transpose(2, 0, 1, 3)
            wd[p] = shd[lo:lo + I_MOE].reshape(IB, 128, D)
        m = {
            "x0": np.ascontiguousarray(x[b]),
            "wq": wq, "wk": wk, "wv": wv, "wo": wo,
            "rt17": np.ascontiguousarray(_tile_w(rt17)),
            "wg": np.ascontiguousarray(wg), "wu": np.ascontiguousarray(wu),
            "wd": np.ascontiguousarray(wd),
            "wproj": wproj, "wo_ssd": wo_ssd,
            "convp": convp, "convb": convb, "dtb": dtb, "alb": alb, "gnw": gnw,
        }
        m.update(cst)
        maps.append(m)
    return maps


class Ctx:
    pass


def _carve(arena, byte_off, dtype, shape):
    esz = {F32: 4, BF16: 2}[dtype]
    nel = 1
    for d in shape[1:]:
        nel *= d
    a = arena[:shape[0], byte_off // 2: byte_off // 2 + (nel * esz) // 2]
    v = a.bitcast(dtype)
    if len(shape) == 2:
        return v
    spec = "p (" + " ".join(f"d{i}" for i in range(1, len(shape))) + ") -> p " + \
        " ".join(f"d{i}" for i in range(1, len(shape)))
    kw = {f"d{i}": shape[i] for i in range(1, len(shape))}
    return v.rearrange(spec, **kw)


def build_program(upto=None, debug=()):
    nc = bass.Bass("TRN2", target_bir_lowering=False, debug=False)
    c = Ctx()
    c.nc = nc

    def din(name, shape, dt=F32):
        return nc.dram_tensor(name, shape, dt, kind="ExternalInput").ap()

    c.x0 = din("x0", [S, D])
    c.wq, c.wk, c.wv, c.wo = (din(n, [DC, 128, D]) for n in ("wq", "wk", "wv", "wo"))
    c.rt17 = din("rt17", [DC, 128, 17])
    c.wg = din("wg", [NPASS, IB, DC, 128, 128])
    c.wu = din("wu", [NPASS, IB, DC, 128, 128])
    c.wd = din("wd", [NPASS, IB, 128, D])
    c.wproj = din("wproj", [DC, 128, PROJ])
    c.wo_ssd = din("wo_ssd", [DC, 128, D])
    c.convp = din("convp", [128, 48])
    c.convb = din("convb", [128, 24])
    c.dtb = din("dtb", [SH, 1])
    c.alb = din("alb", [SH, 1])
    c.gnw = din("gnw", [1, SP])
    c.identb = din("identb", [128, 128], BF16)
    c.identf = din("identf", [128, 128])
    c.onesb = din("onesb", [1, 128], BF16)
    c.onesf = din("onesf", [1, 128])
    c.ropec = din("ropec", [S, 64])
    c.ropes = din("ropes", [S, 64])
    c.maskT = din("maskT", [4, 128, 512])
    c.segneg = din("segneg", [128, 128])
    c.scanmask = din("scanmask", [SH, S])

    c.y = nc.dram_tensor("y", [S, D], F32, kind="ExternalOutput").ap()

    def scratch(name, shape, dt):
        return nc.dram_tensor(name, shape, dt).ap()
    c.xA = scratch("xA", [S, D], F32)
    c.xB = scratch("xB", [S, D], F32)
    c.qfm = scratch("qfm", [AH, 128, S], BF16)
    c.kfm = scratch("kfm", [AH, 128, S], BF16)
    c.vtm = scratch("vtm", [S, D], BF16)
    c.cfm = scratch("cfm", [AH, 128, S], BF16)
    c.act = scratch("act", [NPASS, IB, 128, S], BF16)
    c.part = scratch("part", [S, D], F32)
    c.red = scratch("red", [S, D], F32)
    c.combd = scratch("combd", [17, S], F32)
    c.vkqpre = scratch("vkqpre", [24, 128, S], BF16)
    c.vkqpost = scratch("vkqpost", [24, 128, S], BF16)
    c.gfm = scratch("gfm", [16, 128, S], BF16)
    c.yfm = scratch("yfm", [16, 128, S], BF16)
    c.acsd = scratch("acsd", [SH, S], F32)
    c.ysbd = scratch("ysbd", [128, NCH * 64], F32)
    c.ltd = scratch("ltd", [128, S], F32)
    c.xtmd = scratch("xtmd", [128, NCH * 64], F32)
    c.ydd = scratch("ydd", [128, NCH * 64], F32)
    c.yod = scratch("yod", [128, NCH * 64], F32)
    c.cbd = scratch("cbd", [64, NCH * 64], F32)
    c.dAd = scratch("dAd", [1, 1024], F32)
    dbg = {}
    for name in debug:
        if name == "LNX":
            dbg[name] = nc.dram_tensor("dbg_LNX", [128, DC * S], F32,
                                       kind="ExternalOutput").ap()
        else:
            src = getattr(c, name)
            dbg[name] = nc.dram_tensor("dbg_" + name, list(src.shape),
                                       F32, kind="ExternalOutput").ap()
    c.dbg = dbg

    rec = Rec(SEMS)
    c.r = rec

    with contextlib.ExitStack() as st:
        def sb(name, shape, dt):
            return st.enter_context(nc.sbuf_tensor(name, shape, dt))

        c.LNX = sb("LNX", [128, DC * S], BF16)
        c.W = sb("W", [128, 35840], BF16)
        c.sm = sb("sm", [128, 4096], BF16)
        c.combsg = _carve(c.W, 55296, F32, [17, S])
        c.dttm = _carve(c.W, 40960, F32, [128, 1024])
        c.rowb = _carve(c.W, 16384, F32, [1, S])
        pb01 = st.enter_context(nc.psum_tensor("pb01", [128, 1024], F32))
        c.pb01 = pb01
        c.pb = [pb01[:, 0:512], pb01[:, 512:1024]] + [
            st.enter_context(nc.psum_tensor(f"pb{i}", [128, 512], F32))[:, :]
            for i in range(2, 8)]
        sems = {s: st.enter_context(nc.semaphore(str(s))) for s in SEMS}

        _build_phases(c, upto)

        with nc.Block() as block:
            rec.emit(block, sems)
    return nc


def _sm(c, off, dtype, shape):
    return _carve(c.sm, off, dtype, shape)


def _build_phases(c, upto):
    _p_init(c)
    def attn_wores():
        c.wores_guards = [(SEM_PE, c.attn_done_pe), (SEM_VE, c.attn_done_ve),
                          (SEM_ST, c.attn_done_st)]
        _p_wores(c, c.cfm, c.wo, c.x0, c.xA)

    def ssd_wores():
        c.wores_guards = list(c.ssd_done)
        _p_wores(c, c.yfm, c.wo_ssd, c.xB, c.xA)
    seq = [
        ("LN0", lambda: _p_lnres(c, src=c.x0, add=None, dst=None, ln=True)),
        ("QKV", lambda: _p_attn_qkv(c)),
        ("ATTN", lambda: _p_attn(c)),
        ("WORES_A", attn_wores),
        ("LN1", lambda: _p_lnres(c, src=c.xA, add=None, dst=None, ln=True,
                                 w_guard=c.wores_done[2],
                                 lnx_guard=c.wores_done[:1])),
        ("ROUTER", lambda: _p_router(c)),
        ("F1", lambda: _p_ffn_f1(c)),
        ("F2", lambda: _p_ffn_f2(c)),
        ("AR", lambda: _p_ar(c)),
        ("RES2", lambda: _p_lnres(
            c, src=c.xA, add=c.red, dst=c.xB, ln=True,
            add_gate=lambda t: (SEM_CC, c.ar_cc[t // 8]),
            w_guard=c.f2_done, lnx_guard=[c.f2_done])),
        ("SPROJ", lambda: _p_ssd_proj(c)),
        ("SCONV", lambda: _p_ssd_conv(c)),
        ("SPREP", lambda: _p_ssd_prep(c)),
        ("SHEADS", lambda: _p_ssd_heads(c)),
        ("WORES_S", ssd_wores),
        ("LN3", lambda: _p_lnres(c, src=c.xA, add=None, dst=None, ln=True,
                                 w_guard=c.wores_done[2],
                                 lnx_guard=c.wores_done[:1])),
        ("ROUTER2", lambda: _p_router(c)),
        ("F1B", lambda: _p_ffn_f1(c)),
        ("F2B", lambda: _p_ffn_f2(c)),
        ("AR2", lambda: _p_ar(c)),
        ("OUT", lambda: _p_lnres(
            c, src=c.xA, add=c.red, dst=c.y, ln=False,
            add_gate=lambda t: (SEM_CC, c.ar_cc[t // 8]),
            w_guard=c.f2_done)),
    ]
    for name, fn in seq:
        fn()
        if upto == name:
            break
    _p_debug(c)
    _p_final(c)


def _p_init(c):
    r = c.r
    c.identb_sb = _carve(c.sm, 0, BF16, [128, 128])
    c.onesb_sb = _carve(c.sm, 256, BF16, [1, 128])
    c.identf_sb = _carve(c.sm, 512, F32, [128, 128])
    c.onesf_sb = _carve(c.sm, 1024, F32, [1, 128])
    c.consts = _carve(c.sm, 1536, F32, [128, 16])
    c.convp_sb = _carve(c.sm, 1600, F32, [128, 48])
    c.convb_sb = _carve(c.sm, 1792, F32, [128, 24])
    c.gnwb_sb = _carve(c.sm, 1920, F32, [128, 64])
    c.dtb_sb = _carve(c.sm, 2176, F32, [SH, 1])
    c.alb_sb = _carve(c.sm, 2240, F32, [SH, 1])
    c.gnw_row = _carve(c.sm, 4352, F32, [1, 64])
    c.segneg_sb = _carve(c.sm, 4608, F32, [128, 128])
    r.g.dma(c.identb_sb, c.identb, SEM_LW)
    r.g.dma(c.onesb_sb, c.onesb, SEM_LW)
    r.g.dma(c.identf_sb, c.identf, SEM_LW)
    r.g.dma(c.onesf_sb, c.onesf, SEM_LW)
    r.g.dma(c.convp_sb, c.convp, SEM_LW)
    r.g.dma(c.convb_sb, c.convb, SEM_LW)
    r.g.dma(c.dtb_sb, c.dtb, SEM_LW)
    r.g.dma(c.alb_sb, c.alb, SEM_LW)
    r.g.dma(c.gnw_row, c.gnw, SEM_LW)
    r.g.dma(c.segneg_sb, c.segneg, SEM_LW)
    r.t.wait(SEM_LW, r.n[SEM_LW])
    r.t.matmul(c.pb[0][:, 0:64], c.onesf_sb[:, :], c.gnw_row[0:1, :],
               start=True, stop=True).inc(SEM_PE)
    r.v.wait(SEM_PE, r.n[SEM_PE])
    r.v.tensor_copy(c.gnwb_sb, c.pb[0][:, 0:64]).inc(SEM_VE)
    r.v.memset(c.consts[:, 0:1], LN_EPS)
    r.v.memset(c.consts[:, 1:2], -1e30)
    r.v.memset(c.consts[:, 2:3], GN_EPS)
    r.v.memset(c.consts[:, 3:4], 1.0)
    r.v.memset(c.consts[:, 4:5], 0.0)
    r.v.memset(c.consts[:, 5:6], -1.0)
    r.v.memset(c.consts[:, 6:7], -SHIFT).inc(SEM_VE)
    c.n_init_ve = r.n[SEM_VE]
    c.n_init_lw = r.n[SEM_LW]


def _p_lnres(c, src, add, dst, ln=True, w_off=0, add_gate=None, w_guard=None,
             lnx_guard=None):
    r = c.r
    xin = [_carve(c.W, w_off + i * 8192, F32, [128, D]) for i in range(2)]
    ain = [_carve(c.W, w_off + 16384 + i * 8192, F32, [128, D]) for i in range(2)]
    lno = [_carve(c.W, w_off + 32768 + i * 4096, BF16, [128, D]) for i in range(2)]
    stats = _carve(c.W, w_off + 45056, F32, [128, 16])
    ssum, sumsq, mean, var, rstd, nmr, mm2 = (stats[:, i:i + 1] for i in range(7))
    sq_scr = _carve(c.W, w_off + 46080, BF16, [128, D])

    xv = src.rearrange("(t p) d -> t p d", p=128)
    av = add.rearrange("(t p) d -> t p d", p=128) if add is not None else None
    dv = dst.rearrange("(t p) d -> t p d", p=128) if dst is not None else None

    base_la = r.n[SEM_LA]
    base_lb = r.n[SEM_LB]
    xin_free = [None, None]
    lno_free = [None, None]
    bank_free = {6: None, 7: None}

    for t in range(TT):
        bi = t % 2
        if t < 2 and w_guard is not None:
            r.y.wait(*w_guard)
        if xin_free[bi] is not None:
            sem, val = xin_free[bi]
            r.y.wait(sem, val)
        r.y.dma(xin[bi], xv[t], SEM_LA)
        if add is not None:
            if add_gate is not None:
                g_ = add_gate(t)
                if g_ is not None:
                    r.y.wait(*g_)
            r.y.dma(ain[bi], av[t], SEM_LB)

        r.v.wait(SEM_LA, base_la + 16 * (t + 1))
        if add is not None:
            r.v.wait(SEM_LB, base_lb + 16 * (t + 1))
            r.v.tensor_add(xin[bi][:, :], xin[bi][:, :], ain[bi][:, :]).inc(SEM_VE)
        xn = xin[bi]
        xn_ready = (SEM_VE, r.n[SEM_VE]) if add is not None \
            else (SEM_LA, base_la + 16 * (t + 1))

        if dst is not None:
            if add is not None:
                r.y.wait(*xn_ready)
            r.y.dma(dv[t], xn[:, :], SEM_ST)
            xin_free[bi] = (SEM_ST, r.n[SEM_ST])
        if not ln:
            if dst is None:
                xin_free[bi] = xn_ready
            continue

        mean, negm, mm2n, var2, sd0 = (stats[:, i:i + 1] for i in range(7, 12))
        r.v.reduce_sum(ssum, xn[:, :], axis=AX.X).inc(SEM_VE)
        r.s.wait(*xn_ready)
        r.s.activation(sq_scr[:, :], xn[:, :], AF.Square,
                       accum_out=sumsq).inc(SEM_SC)
        r.s.wait(SEM_SC, r.n[SEM_SC])
        r.s.wait(SEM_VE, r.n[SEM_VE])
        r.s.mul(mean, ssum, 1.0 / D)
        r.s.mul(var, sumsq, 1.0 / D).inc(SEM_SC)
        r.s.wait(SEM_SC, r.n[SEM_SC])
        r.s.mul(negm, mean, -1.0).inc(SEM_SC)
        r.s.wait(SEM_SC, r.n[SEM_SC])
        r.s.mul(mm2n, negm, mean).inc(SEM_SC)
        r.s.wait(SEM_SC, r.n[SEM_SC])
        r.s.add(var2, var, mm2n).inc(SEM_SC)
        r.s.wait(SEM_SC, r.n[SEM_SC])
        r.s.activation(sd0, var2, AF.Sqrt, bias=c.consts[:, 0:1]).inc(SEM_SC)
        r.v.wait(SEM_SC, r.n[SEM_SC])
        r.v.reciprocal(rstd, sd0).inc(SEM_VE)
        r.s.wait(SEM_VE, r.n[SEM_VE])
        r.s.mul(nmr, negm, rstd).inc(SEM_SC)
        r.s.wait(SEM_SC, r.n[SEM_SC])
        if lno_free[bi] is not None:
            r.s.wait(*lno_free[bi])
        r.s.activation(lno[bi][:, :], xn[:, :], AF.Identity,
                       bias=nmr, scale=rstd).inc(SEM_SC)
        xin_free[bi] = (SEM_SC, r.n[SEM_SC])
        lnx_done = r.n[SEM_SC]
        for g4 in range(4):
            bank = 6 + (g4 % 2)
            pv = c.pb[bank][:, :].bitcast(BF16)[:, :512]
            if bank_free[bank] is not None:
                r.t.wait(*bank_free[bank])
            if t == 0 and g4 == 0 and lnx_guard is not None:
                for w_ in lnx_guard:
                    r.v.wait(*w_)
            r.t.wait(SEM_SC, lnx_done)
            for k4 in range(4):
                dc = g4 * 4 + k4
                ins = r.t.transpose(
                    pv.rearrange("p (a b) -> p a b", a=4)[:, k4, :],
                    lno[bi][:, dc * 128:(dc + 1) * 128],
                    c.identb_sb)
            ins.inc(SEM_PE)
            r.v.wait(SEM_PE, r.n[SEM_PE])
            r.v.tensor_copy(_lnx_view(c, g4, t),
                            pv.rearrange("p (a b) -> p a b", a=4)).inc(SEM_VE)
            bank_free[bank] = (SEM_VE, r.n[SEM_VE])
        lno_free[bi] = (SEM_PE, r.n[SEM_PE])
    c.lnres_done = [(SEM_SC, r.n[SEM_SC]), (SEM_PE, r.n[SEM_PE]),
                    (SEM_VE, r.n[SEM_VE]), (SEM_ST, r.n[SEM_ST]),
                    (SEM_LA, r.n[SEM_LA]), (SEM_LB, r.n[SEM_LB])]


def _lnx_view(c, g4, t):
    return c.LNX[:, :].rearrange("p (dc s) -> p dc s", dc=DC)[
        :, g4 * 4:(g4 + 1) * 4, t * 128:(t + 1) * 128]


def _p_router(c):
    r = c.r
    rtw = _carve(c.W, 0, BF16, [128, DC * 17])
    rtv = rtw.rearrange("p (dc j) -> p dc j", dc=DC)
    ew = _carve(c.W, 12288, F32, [128, 16])
    tw = _carve(c.W, 13312, F32, [128, 16])
    kw = _carve(c.W, 14336, F32, [128, 16])
    eq = _carve(c.W, 15360, F32, [128, 16])
    cb = _carve(c.W, 16384, F32, [128, 17])
    sc1 = _carve(c.W, 17472, F32, [128, 8])
    mk, sum4, rc4 = sc1[:, 0:1], sc1[:, 1:2], sc1[:, 2:3]
    lnxv = c.LNX[:, :].rearrange("p (dc s) -> p dc s", dc=DC)

    for w_ in c.lnres_done:
        r.g.wait(*w_)
    r.g.dma(rtv, c.rt17.rearrange("a b c -> b a c"), SEM_LW)
    lw_rt = r.n[SEM_LW]
    bank_free = {0: None, 1: None, 2: None}
    for t in range(TT):
        bk = t % 2
        ps = c.pb[bk]
        if t == 0:
            r.t.wait(SEM_LW, lw_rt)
        if bank_free[bk] is not None:
            r.t.wait(*bank_free[bk])
        for dc in range(DC):
            ins = r.t.matmul(ps[:, 0:17], lnxv[:, dc, t * 128:(t + 1) * 128],
                             rtv[:, dc, :], start=(dc == 0), stop=(dc == DC - 1))
        ins.inc(SEM_PE)
        r.s.wait(SEM_PE, r.n[SEM_PE])
        r.s.wait(SEM_VE, r.n[SEM_VE])
        r.s.activation(ew, ps[:, 0:16], AF.Exp)
        r.s.activation(cb[:, 16:17], ps[:, 16:17], AF.Sigmoid).inc(SEM_SC)
        bank_free[bk] = (SEM_SC, r.n[SEM_SC])
        r.v.wait(SEM_SC, r.n[SEM_SC])
        r.v.tensor_copy(tw, ew).inc(SEM_VE)
        for k in range(4):
            r.v.wait(SEM_VE, r.n[SEM_VE])
            r.v.reduce_max(mk, tw, axis=AX.X).inc(SEM_VE)
            r.v.wait(SEM_VE, r.n[SEM_VE])
            if k == 0:
                r.v.tensor_copy(sum4, mk)
                r.v.tensor_single_scalar(kw, tw, mk, ALU.is_equal).inc(SEM_VE)
                r.v.wait(SEM_VE, r.n[SEM_VE])
                r.v.scalar_tensor_tensor(tw, kw, c.consts[:, 1:2], tw,
                                         ALU.mult, ALU.add).inc(SEM_VE)
            else:
                r.v.tensor_add(sum4, sum4, mk)
                r.v.tensor_single_scalar(eq, tw, mk, ALU.is_equal).inc(SEM_VE)
                r.v.wait(SEM_VE, r.n[SEM_VE])
                r.v.tensor_add(kw, kw, eq)
                r.v.scalar_tensor_tensor(tw, eq, c.consts[:, 1:2], tw,
                                         ALU.mult, ALU.add).inc(SEM_VE)
        r.v.wait(SEM_VE, r.n[SEM_VE])
        r.v.reciprocal(rc4, sum4)
        r.v.tensor_mul(cb[:, 0:16], ew, kw).inc(SEM_VE)
        r.v.wait(SEM_VE, r.n[SEM_VE])
        r.v.tensor_scalar_mul(cb[:, 0:16], cb[:, 0:16], rc4).inc(SEM_VE)
        r.t.wait(SEM_VE, r.n[SEM_VE])
        if bank_free[2] is not None:
            r.t.wait(*bank_free[2])
        r.t.transpose(c.pb[2][:17, :128], cb, c.identf_sb).inc(SEM_PE)
        r.v.wait(SEM_PE, r.n[SEM_PE])
        r.v.tensor_copy(c.combsg[:, t * 128:(t + 1) * 128],
                        c.pb[2][:17, :128]).inc(SEM_VE)
        bank_free[2] = (SEM_VE, r.n[SEM_VE])
    r.y.wait(SEM_VE, r.n[SEM_VE])
    r.y.dma(c.combd, c.combsg[:, :], SEM_ST)
    c.router_done = (SEM_ST, r.n[SEM_ST])


def _p_ffn_f1(c):
    r = c.r
    wgt = [_carve(c.W, 0 + i * 4096, BF16, [128, DC * 128]) for i in range(2)]
    wut = [_carve(c.W, 8192 + i * 4096, BF16, [128, DC * 128]) for i in range(2)]
    scale_sb = _carve(c.W, 16384, BF16, [128, S])
    sgt = [_carve(c.W, 24576 + i * 2048, F32, [128, 512]) for i in range(2)]
    gst = [_carve(c.W, 28672 + i * 2048, F32, [128, 512]) for i in range(2)]
    astg = [_carve(c.W, 32768 + i * 8192, BF16, [128, S]) for i in range(2)]
    lnxv = c.LNX[:, :].rearrange("p (dc s) -> p dc s", dc=DC)

    def load_w(p, ib):
        wb = ib % 2
        if wbuf_free[wb] is not None:
            r.g.wait(*wbuf_free[wb])
        r.g.dma(wgt[wb].rearrange("p (dc j) -> p dc j", dc=DC),
                c.wg[p, ib].rearrange("a b c -> b a c"), SEM_LW)
        r.g.dma(wut[wb].rearrange("p (dc j) -> p dc j", dc=DC),
                c.wu[p, ib].rearrange("a b c -> b a c"), SEM_LW)
        return r.n[SEM_LW]

    wbuf_free = [c.router_done, c.router_done]
    astg_free = [None, None]
    sgt_free = [None, None]
    gst_free = [None, None]
    gate_bank_free = None
    up_bank_free = None
    bcast_bank_free = None
    scale_free = None
    crow_free = None
    c.act_st = {}
    lw_of_ib = {}
    crow = _carve(c.W, 49152, F32, [1, S])
    for p in range(NPASS):
        rs = p if p < 8 else 16
        r.g.wait(*c.router_done)
        if crow_free is not None:
            r.g.wait(*crow_free)
        r.g.dma(crow, c.combd[rs:rs + 1, :], SEM_LB)
        crow_lw = r.n[SEM_LB]
        if up_bank_free is not None:
            r.t.wait(*up_bank_free)
        r.t.wait(SEM_LB, crow_lw)
        for q in range(8):
            if bcast_bank_free is not None:
                r.t.wait(*bcast_bank_free)
            r.t.matmul(c.pb[7][:, :], c.onesf_sb[:, :],
                       crow[0:1, q * 512:(q + 1) * 512],
                       start=True, stop=True).inc(SEM_PE)
            r.v.wait(SEM_PE, r.n[SEM_PE])
            if scale_free is not None:
                r.v.wait(*scale_free)
            r.v.tensor_copy(scale_sb[:, q * 512:(q + 1) * 512],
                            c.pb[7][:, :]).inc(SEM_VE)
            bcast_bank_free = (SEM_VE, r.n[SEM_VE])
        up_bank_free = bcast_bank_free
        crow_free = (SEM_PE, r.n[SEM_PE])
        scale_ready = r.n[SEM_VE]
        if p == 0:
            lw_of_ib[0] = load_w(p, 0)
        for ib in range(IB):
            wb = ib % 2
            lw = lw_of_ib.pop(ib)
            wgv = wgt[wb].rearrange("p (dc j) -> p dc j", dc=DC)
            wuv = wut[wb].rearrange("p (dc j) -> p dc j", dc=DC)
            if ib + 1 < IB:
                lw_of_ib[ib + 1] = load_w(p, ib + 1)
            ab = ib % 2
            for tcg in range(2):
                r.t.wait(SEM_LW, lw)
                if gate_bank_free is not None:
                    r.t.wait(*gate_bank_free)
                for dc in range(DC):
                    for t4 in range(4):
                        tc = tcg * 4 + t4
                        ins = r.t.matmul(c.pb[t4][:, :], wgv[:, dc, :],
                                         lnxv[:, dc, tc * 512:(tc + 1) * 512],
                                         start=(dc == 0), stop=(dc == DC - 1))
                ins.inc(SEM_PE)
                gate_done = r.n[SEM_PE]
                if up_bank_free is not None:
                    r.t.wait(*up_bank_free)
                for dc in range(DC):
                    for t4 in range(4):
                        tc = tcg * 4 + t4
                        ins = r.t.matmul(c.pb[4 + t4][:, :], wuv[:, dc, :],
                                         lnxv[:, dc, tc * 512:(tc + 1) * 512],
                                         start=(dc == 0), stop=(dc == DC - 1))
                ins.inc(SEM_PE)
                up_done = r.n[SEM_PE]
                for t4 in range(4):
                    tc = tcg * 4 + t4
                    eb = t4 % 2
                    if sgt_free[eb] is not None:
                        r.s.wait(*sgt_free[eb])
                    r.s.wait(SEM_PE, gate_done)
                    r.s.activation(sgt[eb][:, :], c.pb[t4][:, :],
                                   AF.Silu).inc(SEM_SC)
                    if gst_free[eb] is not None:
                        r.g.wait(*gst_free[eb])
                    r.g.wait(SEM_VE, scale_ready)
                    r.g.wait(SEM_SC, r.n[SEM_SC])
                    r.g.tensor_mul(gst[eb][:, :], sgt[eb][:, :],
                                   scale_sb[:, tc * 512:(tc + 1) * 512]).inc(SEM_GP)
                    sgt_free[eb] = (SEM_GP, r.n[SEM_GP])
                    if tcg == 0 and t4 == 0 and astg_free[ab] is not None:
                        r.v.wait(*astg_free[ab])
                    r.v.wait(SEM_GP, r.n[SEM_GP])
                    r.v.wait(SEM_PE, up_done)
                    r.v.tensor_mul(astg[ab][:, tc * 512:(tc + 1) * 512],
                                   gst[eb][:, :], c.pb[4 + t4][:, :]).inc(SEM_VE)
                    gst_free[eb] = (SEM_VE, r.n[SEM_VE])
                gate_bank_free = (SEM_SC, r.n[SEM_SC])
                up_bank_free = (SEM_VE, r.n[SEM_VE])
            scale_free = (SEM_GP, r.n[SEM_GP])
            r.y.wait(SEM_VE, r.n[SEM_VE])
            r.y.dma(c.act[p, ib], astg[ab][:, :], SEM_ST)
            astg_free[ab] = (SEM_ST, r.n[SEM_ST])
            wbuf_free[wb] = (SEM_PE, up_done)
        if p + 1 < NPASS:
            lw_of_ib[0] = load_w(p + 1, 0)
        c.act_st[p] = r.n[SEM_ST]
    c.f1_done_pe = r.n[SEM_PE]
    c.f1_done_ve = r.n[SEM_VE]


def _p_ffn_f2(c):
    r = c.r
    wdt = [[_carve(c.LNX, (e2 * IB + ib) * 4096, BF16, [128, D])
            for ib in range(IB)] for e2 in range(2)]
    actt = [[_carve(c.W, (e2 * IB + ib) * 2048, BF16, [128, 1024])
             for ib in range(IB)] for e2 in range(2)]
    stg = [_carve(c.W, 45056, F32, [128, 2048]),
           _carve(c.LNX, 90112, F32, [128, 2048])]
    partv = c.part.rearrange("(t p) d -> t p d", p=128)

    part_seq = {}
    wd_busy_until = (SEM_PE, c.f1_done_pe)
    g0 = [(SEM_ST, None), (SEM_VE, None)]
    actbuf_free = [None, None]
    stage_free = [None, None]
    psum_free = {0: None, 4: None}
    for ep in range(5):
        r.g.wait(*wd_busy_until)
        for e2 in range(2):
            for ib in range(IB):
                r.g.dma(wdt[e2][ib], c.wd[2 * ep + e2, ib], SEM_LW)
        wd_lw = r.n[SEM_LW]
        for q in range(4):
            for e2 in range(2):
                if ep == 0 and q == 0:
                    r.g.wait(SEM_ST, c.act_st[NPASS - 1])
                    r.g.wait(SEM_VE, c.f1_done_ve)
                if actbuf_free[e2] is not None:
                    r.g.wait(*actbuf_free[e2])
                r.g.wait(SEM_ST, c.act_st[2 * ep + e2])
                r.g.dma(_f2_act_view(c, e2),
                        c.act[2 * ep + e2][:, :, q * 1024:(q + 1) * 1024]
                        .rearrange("ib p n -> p ib n"),
                        SEM_LA)
            act_lw = r.n[SEM_LA]
            for tt8 in range(8):
                t = q * 8 + tt8
                grp = (tt8 % 2) * 4
                r.t.wait(SEM_LW, wd_lw)
                r.t.wait(SEM_LA, act_lw)
                if psum_free[grp] is not None:
                    for w_ in psum_free[grp]:
                        r.t.wait(*w_)
                for e2 in range(2):
                    for ib in range(IB):
                        lhsT = actt[e2][ib][:, tt8 * 128:(tt8 + 1) * 128]
                        for of in range(4):
                            ins = r.t.matmul(
                                c.pb[grp + of][:, :], lhsT,
                                wdt[e2][ib][:, of * 512:(of + 1) * 512],
                                start=(e2 == 0 and ib == 0),
                                stop=(e2 == 1 and ib == IB - 1))
                ins.inc(SEM_PE)
                mm_done = r.n[SEM_PE]
                sb_ = (tt8 % 2)
                if stage_free[sb_] is not None:
                    r.v.wait(*stage_free[sb_])
                    r.s.wait(*stage_free[sb_])
                r.v.wait(SEM_PE, mm_done)
                r.s.wait(SEM_PE, mm_done)
                r.v.tensor_copy(stg[sb_][:, 0:512], c.pb[grp + 0][:, :])
                r.v.tensor_copy(stg[sb_][:, 512:1024], c.pb[grp + 1][:, :]).inc(SEM_VE)
                r.s.copy(stg[sb_][:, 1024:1536], c.pb[grp + 2][:, :])
                r.s.copy(stg[sb_][:, 1536:2048], c.pb[grp + 3][:, :]).inc(SEM_SC)
                psum_free[grp] = [(SEM_VE, r.n[SEM_VE]), (SEM_SC, r.n[SEM_SC])]
                r.g.wait(SEM_VE, r.n[SEM_VE])
                r.g.wait(SEM_SC, r.n[SEM_SC])
                if ("t", t) in part_seq:
                    r.g.wait(*part_seq[("t", t)])
                r.g.dma(partv[t], stg[sb_][:, :], SEM_AD,
                        accum_op=(ALU.add if ep > 0 else ALU.bypass))
                part_seq[("t", t)] = (SEM_AD, r.n[SEM_AD])
                stage_free[sb_] = (SEM_AD, r.n[SEM_AD])
            pe_after_q = r.n[SEM_PE]
            actbuf_free = [(SEM_PE, pe_after_q), (SEM_PE, pe_after_q)]
        wd_busy_until = (SEM_PE, r.n[SEM_PE])
    c.part_seq = part_seq
    c.f2_done = (SEM_AD, r.n[SEM_AD])


def _f2_act_view(c, e2):
    v = _carve(c.W, e2 * IB * 2048, BF16, [128, IB * 1024])
    return v.rearrange("p (ib n) -> p ib n", ib=IB)


def _p_ar(c):
    r = c.r
    rows = S // 4
    c.ar_cc = []
    r.g.wait(SEM_LB, r.n[SEM_LB])
    for q in range(4):
        last = c.part_seq[("t", q * 8 + 7)]
        r.g.wait(*last)
        r.g.collective_compute(
            "AllReduce", ALU.add, replica_groups=RG_PAIRS,
            ins=[c.part[q * rows:(q + 1) * rows, :].opt()],
            outs=[c.red[q * rows:(q + 1) * rows, :].opt()]).inc(SEM_CC)
        c.ar_cc.append(r.n[SEM_CC])


def _p_attn_qkv(c):
    r = c.r
    wh = _carve(c.W, 0, BF16, [128, DC * 1024])
    whv = wh.rearrange("p (dc j) -> p dc j", dc=DC)
    rtmp = _carve(c.W, 32768, F32, [128, 1024])
    rop = [_carve(c.W, 36864 + i * 2048, BF16, [128, 1024]) for i in range(2)]
    trs = [_carve(c.W, 40960 + i * 1024, BF16, [128, 512]) for i in range(2)]
    vt_ = [_carve(c.W, 45056 + i * 2048, BF16, [128, 1024]) for i in range(2)]
    cs_ = [_carve(c.W, 49152 + i * 1024, F32, [128, 64]) for i in range(4)]
    t14 = [_carve(c.W, 53248 + i * 2048, F32, [128, 512]) for i in range(4)]
    lnxv = c.LNX[:, :].rearrange("p (dc s) -> p dc s", dc=DC)

    guards = list(c.lnres_done)
    for g_ in guards:
        r.y.wait(*g_)
    wsrc = {"q": c.wq, "k": c.wk, "v": c.wv}
    dfm = {"q": c.qfm, "k": c.kfm}
    w_free = None
    rtmp_free = None
    rop_free = [None, None]
    trs_free = [None, None]
    vt_free = [None, None]
    cs_free = [None, None]
    t14_free = [None, None, None, None]
    mm_banks_free = {0: None, 2: None}
    tp_bank_free = {4: None, 5: None}
    for pj in ("q", "k", "v"):
        for hf in range(2):
            for g_ in guards:
                r.g.wait(*g_)
            if w_free is not None:
                r.g.wait(*w_free)
            r.g.dma(whv, wsrc[pj][:, :, hf * 1024:(hf + 1) * 1024]
                    .rearrange("a b c -> b a c"), SEM_LW)
            guards = []
            lw = r.n[SEM_LW]
            for t in range(TT):
                cb = t % 2
                if pj != "v":
                    if cs_free[cb] is not None:
                        r.y.wait(*cs_free[cb])
                    r.y.dma(cs_[cb], c.ropec[t * 128:(t + 1) * 128, :], SEM_LA)
                    r.y.dma(cs_[2 + cb], c.ropes[t * 128:(t + 1) * 128, :], SEM_LA)
                la = r.n[SEM_LA]
                bk = (t % 2) * 2
                r.t.wait(SEM_LW, lw)
                if mm_banks_free[bk] is not None:
                    r.t.wait(*mm_banks_free[bk])
                for dc in range(DC):
                    for n2 in range(2):
                        ins = r.t.matmul(c.pb[bk + n2][:, :],
                                         lnxv[:, dc, t * 128:(t + 1) * 128],
                                         whv[:, dc, n2 * 512:(n2 + 1) * 512],
                                         start=(dc == 0), stop=(dc == DC - 1))
                ins.inc(SEM_PE)
                mmdone = r.n[SEM_PE]
                if pj == "v":
                    vb = t % 2
                    if vt_free[vb] is not None:
                        r.v.wait(*vt_free[vb])
                    r.v.wait(SEM_PE, mmdone)
                    r.v.tensor_copy(vt_[vb][:, 0:512], c.pb[bk][:, :])
                    r.v.tensor_copy(vt_[vb][:, 512:1024], c.pb[bk + 1][:, :]).inc(SEM_VE)
                    mm_banks_free[bk] = (SEM_VE, r.n[SEM_VE])
                    r.y.wait(SEM_VE, r.n[SEM_VE])
                    r.y.dma(c.vtm[t * 128:(t + 1) * 128,
                                  hf * 1024:(hf + 1) * 1024], vt_[vb][:, :], SEM_ST)
                    vt_free[vb] = (SEM_ST, r.n[SEM_ST])
                    continue
                rb = t % 2
                if rtmp_free is not None:
                    r.v.wait(*rtmp_free)
                r.v.wait(SEM_PE, mmdone)
                r.v.tensor_copy(rtmp[:, 0:512], c.pb[bk][:, :])
                r.v.tensor_copy(rtmp[:, 512:1024], c.pb[bk + 1][:, :]).inc(SEM_VE)
                mm_banks_free[bk] = (SEM_VE, r.n[SEM_VE])
                ev = rtmp.rearrange("p (h z) -> p h z", h=8)[:, :, 0:64]
                ov = rtmp.rearrange("p (h z) -> p h z", h=8)[:, :, 64:128]
                cosb = cs_[cb][:, 0:64].unsqueeze(1).broadcast_to([128, 8, 64])
                sinb = cs_[2 + cb][:, 0:64].unsqueeze(1).broadcast_to([128, 8, 64])
                r.g.wait(SEM_VE, r.n[SEM_VE])
                r.g.wait(SEM_LA, la)
                for i4 in range(4):
                    if t14_free[i4] is not None:
                        r.g.wait(*t14_free[i4])
                t1v = [t14[i].rearrange("p (h z) -> p h z", h=8) for i in range(4)]
                r.g.tensor_mul(t1v[0], ev, cosb)
                r.g.tensor_mul(t1v[1], ov, sinb)
                r.g.tensor_mul(t1v[2], ev, sinb)
                r.g.tensor_mul(t1v[3], ov, cosb).inc(SEM_GP)
                r.g.wait(SEM_GP, r.n[SEM_GP])
                if rop_free[rb] is not None:
                    r.g.wait(*rop_free[rb])
                rv = rop[rb].rearrange("p (h z) -> p h z", h=8)
                r.g.tensor_sub(rv[:, :, 0:64], t1v[0], t1v[1])
                r.g.tensor_add(rv[:, :, 64:128], t1v[2], t1v[3]).inc(SEM_GP)
                rtmp_free = (SEM_GP, r.n[SEM_GP])
                for i4 in range(4):
                    t14_free[i4] = (SEM_GP, r.n[SEM_GP])
                for g4 in range(2):
                    tb = 4 + g4
                    pv = c.pb[tb][:, :].bitcast(BF16)[:, :512]
                    if tp_bank_free[tb] is not None:
                        r.t.wait(*tp_bank_free[tb])
                    r.t.wait(SEM_GP, r.n[SEM_GP])
                    for k4 in range(4):
                        hh = g4 * 4 + k4
                        ins = r.t.transpose(
                            pv.rearrange("p (a b) -> p a b", a=4)[:, k4, :],
                            rop[rb][:, hh * 128:(hh + 1) * 128],
                            c.identb_sb)
                    ins.inc(SEM_PE)
                    tr = trs[g4]
                    if trs_free[g4] is not None:
                        r.v.wait(*trs_free[g4])
                    r.v.wait(SEM_PE, r.n[SEM_PE])
                    r.v.tensor_copy(tr.rearrange("p (a b) -> p a b", a=4),
                                    pv.rearrange("p (a b) -> p a b", a=4)).inc(SEM_VE)
                    tp_bank_free[tb] = (SEM_VE, r.n[SEM_VE])
                    r.y.wait(SEM_VE, r.n[SEM_VE])
                    r.y.dma(dfm[pj][hf * 8 + g4 * 4: hf * 8 + g4 * 4 + 4, :,
                                    t * 128:(t + 1) * 128]
                            .rearrange("h p n -> p h n"),
                            tr.rearrange("p (h n) -> p h n", h=4), SEM_ST)
                    trs_free[g4] = (SEM_ST, r.n[SEM_ST])
                rop_free[rb] = (SEM_PE, r.n[SEM_PE])
            w_free = (SEM_PE, r.n[SEM_PE])
    c.qkv_done_st = r.n[SEM_ST]
    c.qkv_done_pe = r.n[SEM_PE]


def _p_attn(c):
    r = c.r
    qb = [_carve(c.LNX, i * 8192, BF16, [128, S]) for i in range(2)]
    kb = [_carve(c.LNX, 16384 + i * 8192, BF16, [128, S]) for i in range(2)]
    vb = [_carve(c.LNX, 32768 + i * 8192, BF16, [128, S]) for i in range(2)]
    pt = [_carve(c.LNX, 49152 + i * 1024, BF16, [128, 512]) for i in range(2)]
    rcp = _carve(c.LNX, 51200, F32, [1, 512])
    rcpb = _carve(c.LNX, 53248, F32, [128, 512])
    cxo = [_carve(c.LNX, 55296 + i * 1024, BF16, [128, 512]) for i in range(2)]
    msk = _carve(c.LNX, 57344, F32, [128, 4 * 512])
    onescol = _carve(c.LNX, 65536, BF16, [128, 1])
    sco = c.pb01[:, :]

    r.g.wait(SEM_PE, c.qkv_done_pe)
    r.g.wait(SEM_ST, c.qkv_done_st)
    r.g.dma(msk.rearrange("p (a b) -> p a b", a=4),
            c.maskT.rearrange("a b c -> b a c"), SEM_LW)
    r.g.memset(onescol[:, :], 1.0).inc(SEM_GP)
    mask_lw = r.n[SEM_LW]
    ones_gp = r.n[SEM_GP]

    r.y.wait(SEM_PE, c.qkv_done_pe)
    r.y.wait(SEM_ST, c.qkv_done_st)
    hbuf_free = [None, None]
    lw_of_h = {}

    def load_h(h):
        hb = h % 2
        if hbuf_free[hb] is not None:
            r.y.wait(*hbuf_free[hb])
        r.y.dma(qb[hb][:, :], c.qfm[h], SEM_LA)
        r.y.dma(kb[hb][:, :], c.kfm[h], SEM_LA)
        r.y.dma(vb[hb][:, :].rearrange("p (kt n) -> p kt n", kt=TT),
                c.vtm.rearrange("(kt p) d -> p kt d", p=128)
                [:, :, h * 128:(h + 1) * 128], SEM_LA)
        return r.n[SEM_LA]

    lw_of_h[0] = load_h(0)
    pt_free = [None, None]
    ctx_bank_free = {2: None, 3: None}
    sum_bank_free = {4: None, 5: None}
    bc_bank_free = {6: None, 7: None}
    rcp_free = None
    rcpb_free = None
    cxo_free = [None, None]
    for h in range(AH):
        hb = h % 2
        la = lw_of_h.pop(h)
        if h + 1 < AH:
            lw_of_h[h + 1] = load_h(h + 1)
        for qg in range(8):
            cbk = 2 + (qg % 2)
            sbk = 4 + (qg % 2)
            nkt = (qg + 1) * 4
            exp_of_kt = {}
            for kt in range(nkt + 1):
                if kt < nkt:
                    r.t.wait(SEM_LA, la)
                    half = kt % 2
                    if kt >= 2:
                        r.t.wait(SEM_SC, exp_of_kt[kt - 2])
                    r.t.matmul(sco[:, half * 512:(half + 1) * 512],
                               kb[hb][:, kt * 128:(kt + 1) * 128],
                               qb[hb][:, qg * 512:(qg + 1) * 512],
                               start=True, stop=True).inc(SEM_PE)
                    sc_mm = r.n[SEM_PE]
                    if kt >= qg * 4:
                        j = kt - qg * 4
                        r.v.wait(SEM_PE, sc_mm)
                        if h == 0 and qg == 0 and kt == 0:
                            r.v.wait(SEM_LW, mask_lw)
                        r.v.tensor_add(sco[:, half * 512:(half + 1) * 512],
                                       sco[:, half * 512:(half + 1) * 512],
                                       msk[:, j * 512:(j + 1) * 512]).inc(SEM_VE)
                        gate = (SEM_VE, r.n[SEM_VE])
                    else:
                        gate = (SEM_PE, sc_mm)
                    if pt_free[half] is not None:
                        r.s.wait(*pt_free[half])
                    r.s.wait(*gate)
                    r.s.activation(pt[half][:, :],
                                   sco[:, half * 512:(half + 1) * 512],
                                   AF.Exp, bias=c.consts[:, 6:7],
                                   scale=ASCALE).inc(SEM_SC)
                    exp_of_kt[kt] = r.n[SEM_SC]
                ct = kt - 1
                if ct >= 0:
                    half = ct % 2
                    r.t.wait(SEM_SC, exp_of_kt[ct])
                    if ct == 0:
                        if ctx_bank_free[cbk] is not None:
                            r.t.wait(*ctx_bank_free[cbk])
                        if sum_bank_free[sbk] is not None:
                            r.t.wait(*sum_bank_free[sbk])
                        if h == 0 and qg == 0:
                            r.t.wait(SEM_GP, ones_gp)
                    r.t.matmul(c.pb[cbk][:, :],
                               vb[hb][:, ct * 128:(ct + 1) * 128],
                               pt[half][:, :],
                               start=(ct == 0), stop=(ct == nkt - 1))
                    r.t.matmul(c.pb[sbk][:1, :], onescol[:, :],
                               pt[half][:, :],
                               start=(ct == 0),
                               stop=(ct == nkt - 1)).inc(SEM_PE)
                    pt_free[half] = (SEM_PE, r.n[SEM_PE])
            ctx_done = r.n[SEM_PE]
            if rcp_free is not None:
                r.v.wait(*rcp_free)
            r.v.wait(SEM_PE, ctx_done)
            r.v.reciprocal(rcp[:, :], c.pb[sbk][:1, :]).inc(SEM_VE)
            sum_bank_free[sbk] = (SEM_VE, r.n[SEM_VE])
            bbk = 6 + (qg % 2)
            if bc_bank_free[bbk] is not None:
                r.t.wait(*bc_bank_free[bbk])
            r.t.wait(SEM_VE, r.n[SEM_VE])
            r.t.matmul(c.pb[bbk][:, :], c.onesf_sb[:, :], rcp[:, :],
                       start=True, stop=True).inc(SEM_PE)
            rcp_free = (SEM_PE, r.n[SEM_PE])
            if rcpb_free is not None:
                r.v.wait(*rcpb_free)
            r.v.wait(SEM_PE, r.n[SEM_PE])
            r.v.tensor_copy(rcpb[:, :], c.pb[bbk][:, :]).inc(SEM_VE)
            bc_bank_free[bbk] = (SEM_VE, r.n[SEM_VE])
            co = cxo[qg % 2]
            if cxo_free[qg % 2] is not None:
                r.v.wait(*cxo_free[qg % 2])
            r.v.wait(SEM_VE, r.n[SEM_VE])
            r.v.tensor_mul(co[:, :], c.pb[cbk][:, :], rcpb[:, :]).inc(SEM_VE)
            rcpb_free = (SEM_VE, r.n[SEM_VE])
            ctx_bank_free[cbk] = (SEM_VE, r.n[SEM_VE])
            r.y.wait(SEM_VE, r.n[SEM_VE])
            r.y.dma(c.cfm[h][:, qg * 512:(qg + 1) * 512], co[:, :], SEM_ST)
            cxo_free[qg % 2] = (SEM_ST, r.n[SEM_ST])
        hbuf_free[hb] = (SEM_PE, r.n[SEM_PE])
    c.attn_done_pe = r.n[SEM_PE]
    c.attn_done_st = r.n[SEM_ST]
    c.attn_done_ve = r.n[SEM_VE]


def _p_wores(c, srcfm, wsrc, xsrc, xdst, nhb=16):
    r = c.r
    wot = [_carve(c.LNX, hb * 4096, BF16, [128, D]) for hb in range(nhb)]
    ctile = [_carve(c.W, i * 4096, BF16, [128, nhb * 128]) for i in range(2)]
    xtl = [_carve(c.W, 16384 + i * 8192, F32, [128, D]) for i in range(2)]
    stg = [_carve(c.W, 32768 + i * 8192, F32, [128, D]) for i in range(2)]

    for g_ in c.wores_guards:
        r.g.wait(*g_)
    for hb in range(nhb):
        r.g.dma(wot[hb], wsrc[hb], SEM_LW)
    lw = r.n[SEM_LW]
    xv = xsrc.rearrange("(t p) d -> t p d", p=128)
    dv = xdst.rearrange("(t p) d -> t p d", p=128)
    ct_free = [None, None]
    xt_free = [None, None]
    st_free = [None, None]
    bank_free = {0: None, 4: None}
    for t in range(TT):
        bi = t % 2
        if ct_free[bi] is not None:
            r.y.wait(*ct_free[bi])
        r.y.dma(ctile[bi].rearrange("p (h n) -> p h n", h=nhb),
                srcfm[:, :, t * 128:(t + 1) * 128].rearrange("h p n -> p h n"),
                SEM_LA)
        if xt_free[bi] is not None:
            r.y.wait(*xt_free[bi])
        r.y.dma(xtl[bi], xv[t], SEM_LB)
        la, lb = r.n[SEM_LA], r.n[SEM_LB]
        grp = (t % 2) * 4
        r.t.wait(SEM_LA, la)
        if t == 0:
            r.t.wait(SEM_LW, lw)
        if bank_free[grp] is not None:
            for w_ in bank_free[grp]:
                r.t.wait(*w_)
        cv = ctile[bi].rearrange("p (h n) -> p h n", h=nhb)
        for hb in range(nhb):
            for of in range(4):
                ins = r.t.matmul(c.pb[grp + of][:, :], cv[:, hb, :],
                                 wot[hb][:, of * 512:(of + 1) * 512],
                                 start=(hb == 0), stop=(hb == nhb - 1))
        ins.inc(SEM_PE)
        mm = r.n[SEM_PE]
        if st_free[bi] is not None:
            r.v.wait(*st_free[bi])
        r.v.wait(SEM_PE, mm)
        r.v.wait(SEM_LB, lb)
        for of in range(4):
            ins = r.v.tensor_add(stg[bi][:, of * 512:(of + 1) * 512],
                                 xtl[bi][:, of * 512:(of + 1) * 512],
                                 c.pb[grp + of][:, :])
        ins.inc(SEM_VE)
        bank_free[grp] = [(SEM_VE, r.n[SEM_VE])]
        xt_free[bi] = (SEM_VE, r.n[SEM_VE])
        ct_free[bi] = (SEM_PE, mm)
        r.y.wait(SEM_VE, r.n[SEM_VE])
        r.y.dma(dv[t], stg[bi][:, :], SEM_ST)
        st_free[bi] = (SEM_ST, r.n[SEM_ST])
    c.wores_done = [(SEM_PE, r.n[SEM_PE]), (SEM_VE, r.n[SEM_VE]),
                    (SEM_ST, r.n[SEM_ST])]


def _p_ssd_proj(c):
    r = c.r
    wpj = [_carve(c.W, i * 16384, BF16, [128, DC * 512]) for i in range(2)]
    stg = [_carve(c.W, 32768 + i * 1024, BF16, [128, 512]) for i in range(2)]
    trs = [_carve(c.W, 34816 + i * 1024, BF16, [128, 512]) for i in range(2)]
    guards = list(c.lnres_done)
    for g_ in guards:
        r.g.wait(*g_)
        r.y.wait(*g_)
    lnxv = c.LNX[:, :].rearrange("p (dc s) -> p dc s", dc=DC)
    wbuf_free = [None, None]
    stg_free = [None, None]
    trs_free = [None, None]
    mmb_free = {0: None, 1: None}
    tpb_free = {4: None, 5: None}
    dt_written = 0
    for cg in range(11):
        wb = cg % 2
        ncols = 512 if cg < 10 else 32
        if wbuf_free[wb] is not None:
            r.g.wait(*wbuf_free[wb])
        r.g.dma(wpj[wb].rearrange("p (dc j) -> p dc j", dc=DC)[:, :, 0:ncols],
                c.wproj[:, :, cg * 512:cg * 512 + ncols]
                .rearrange("a b c -> b a c"), SEM_LW)
        lw = r.n[SEM_LW]
        wv = wpj[wb].rearrange("p (dc j) -> p dc j", dc=DC)
        for t in range(TT):
            bk = t % 2
            r.t.wait(SEM_LW, lw)
            if mmb_free[bk] is not None:
                r.t.wait(*mmb_free[bk])
            for dc in range(DC):
                ins = r.t.matmul(c.pb[bk][:, 0:ncols],
                                 lnxv[:, dc, t * 128:(t + 1) * 128],
                                 wv[:, dc, 0:ncols],
                                 start=(dc == 0), stop=(dc == DC - 1))
            ins.inc(SEM_PE)
            mm = r.n[SEM_PE]
            if cg == 10:
                r.v.wait(SEM_PE, mm)
                r.v.tensor_copy(c.dttm[:, t * 32:(t + 1) * 32],
                                c.pb[bk][:, 0:32]).inc(SEM_VE)
                mmb_free[bk] = (SEM_VE, r.n[SEM_VE])
                dt_written = r.n[SEM_VE]
                continue
            sb_ = t % 2
            if stg_free[sb_] is not None:
                r.v.wait(*stg_free[sb_])
            r.v.wait(SEM_PE, mm)
            r.v.tensor_copy(stg[sb_][:, :], c.pb[bk][:, 0:512]).inc(SEM_VE)
            mmb_free[bk] = (SEM_VE, r.n[SEM_VE])
            tb = 4 + (t % 2)
            pv = c.pb[tb][:, :].bitcast(BF16)[:, :512]
            if tpb_free[tb] is not None:
                r.t.wait(*tpb_free[tb])
            r.t.wait(SEM_VE, r.n[SEM_VE])
            for k4 in range(4):
                ins = r.t.transpose(
                    pv.rearrange("p (a b) -> p a b", a=4)[:, k4, :],
                    stg[sb_][:, k4 * 128:(k4 + 1) * 128], c.identb_sb)
            ins.inc(SEM_PE)
            stg_free[sb_] = (SEM_PE, r.n[SEM_PE])
            if trs_free[sb_] is not None:
                r.v.wait(*trs_free[sb_])
            r.v.wait(SEM_PE, r.n[SEM_PE])
            r.v.tensor_copy(trs[sb_].rearrange("p (a b) -> p a b", a=4),
                            pv.rearrange("p (a b) -> p a b", a=4)).inc(SEM_VE)
            tpb_free[tb] = (SEM_VE, r.n[SEM_VE])
            dst = c.vkqpre if cg < 6 else c.gfm
            blk0 = cg * 4 if cg < 6 else (cg - 6) * 4
            r.y.wait(SEM_VE, r.n[SEM_VE])
            r.y.dma(dst[blk0:blk0 + 4, :, t * 128:(t + 1) * 128]
                    .rearrange("h p n -> p h n"),
                    trs[sb_].rearrange("p (h n) -> p h n", h=4), SEM_ST)
            trs_free[sb_] = (SEM_ST, r.n[SEM_ST])
        wbuf_free[wb] = (SEM_PE, r.n[SEM_PE])
    c.proj_done = [(SEM_PE, r.n[SEM_PE]), (SEM_ST, r.n[SEM_ST]),
                   (SEM_VE, dt_written)]


def _p_ssd_conv(c):
    r = c.r
    cv = [_carve(c.W, i * 8192, BF16, [128, S]) for i in range(2)]
    t1 = _carve(c.W, 16384, F32, [128, S])
    t2 = _carve(c.W, 45056, F32, [128, S])
    ob = [_carve(c.W, 32768, BF16, [128, S]),
          _carve(c.W, 61440, BF16, [128, S])]

    for g_ in c.proj_done:
        r.y.wait(*g_)
    cv_free = [None, None]
    t_free = None
    ob_free = [None, None]
    for cb in range(24):
        bi = cb % 2
        if cv_free[bi] is not None:
            r.y.wait(*cv_free[bi])
        r.y.dma(cv[bi][:, :], c.vkqpre[cb], SEM_LA)
        la = r.n[SEM_LA]
        w0 = c.convp_sb[:, 2 * cb:2 * cb + 1]
        w1 = c.convp_sb[:, 2 * cb + 1:2 * cb + 2]
        b_ = c.convb_sb[:, cb:cb + 1]
        r.v.wait(SEM_LA, la)
        if t_free is not None:
            r.v.wait(*t_free)
        r.v.tensor_scalar(t1[:, :], cv[bi][:, :], w1, b_,
                          ALU.mult, ALU.add).inc(SEM_VE)
        r.v.wait(SEM_VE, r.n[SEM_VE])
        r.v.scalar_tensor_tensor(t2[:, 1:S], cv[bi][:, 0:S - 1], w0,
                                 t1[:, 1:S], ALU.mult, ALU.add)
        r.v.tensor_copy(t2[:, 0:1], t1[:, 0:1]).inc(SEM_VE)
        cv_free[bi] = (SEM_VE, r.n[SEM_VE])
        if ob_free[bi] is not None:
            r.s.wait(*ob_free[bi])
        r.s.wait(SEM_VE, r.n[SEM_VE])
        r.s.activation(ob[bi][:, :], t2[:, :], AF.Silu).inc(SEM_SC)
        t_free = (SEM_SC, r.n[SEM_SC])
        r.y.wait(SEM_SC, r.n[SEM_SC])
        r.y.dma(c.vkqpost[cb], ob[bi][:, :], SEM_ST)
        ob_free[bi] = (SEM_ST, r.n[SEM_ST])
    c.conv_done = [(SEM_SC, r.n[SEM_SC]), (SEM_ST, r.n[SEM_ST]),
                   (SEM_VE, r.n[SEM_VE])]


PREP_CUT = None
SSD_DEV = None


def _p_ssd_prep(c):
    r = c.r
    L = c.LNX
    dtfm = _carve(L, 0, F32, [SH, S])
    sp1 = _carve(L, 16384, F32, [SH, S])
    sp2 = _carve(L, 96 * 1024, F32, [SH, S])
    acs = _carve(L, 32768, F32, [SH, S])
    eacs = _carve(L, 49152, F32, [SH, S])
    dstf = _carve(L, 65536, F32, [SH, S])
    smask = _carve(L, 80 * 1024, F32, [SH, S])
    acst = _carve(c.W, 45056, F32, [128, 1024])
    eacst = _carve(c.W, 49152, F32, [128, 1024])
    dstt = _carve(c.W, 53248, F32, [128, 1024])
    dtvt = _carve(c.W, 57344, F32, [128, 1024])
    dab = _carve(c.W, 61440, F32, [128, 1024])
    expal = _carve(c.sm, 2304, F32, [SH, 1])
    for g_ in c.proj_done + c.conv_done:
        r.g.wait(*g_)
    r.g.dma(smask, c.scanmask, SEM_LA)
    for t in range(TT):
        bk = t % 2
        if t >= 2:
            r.t.wait(SEM_VE, ve_evac[t - 2])
        if t == 0:
            for g_ in c.proj_done + c.conv_done:
                r.t.wait(*g_)
            ve_evac = {}
        r.t.transpose(c.pb[bk][:32, :128], c.dttm[:, t * 32:(t + 1) * 32],
                      c.identf_sb).inc(SEM_PE)
        r.v.wait(SEM_PE, r.n[SEM_PE])
        r.v.tensor_copy(dtfm[:, t * 128:(t + 1) * 128],
                        c.pb[bk][:32, :128]).inc(SEM_VE)
        ve_evac[t] = r.n[SEM_VE]
    if PREP_CUT == "T":
        return
    r.v.wait(SEM_VE, r.n[SEM_VE])
    r.v.tensor_scalar(sp1, dtfm, c.dtb_sb[:, 0:1], None, ALU.add).inc(SEM_VE)
    r.s.wait(SEM_VE, r.n[SEM_VE])
    r.s.activation(sp2, sp1, AF.Abs).inc(SEM_SC)
    r.s.wait(SEM_SC, r.n[SEM_SC])
    r.s.activation(sp2, sp2, AF.Exp, scale=-1.0).inc(SEM_SC)
    r.s.wait(SEM_SC, r.n[SEM_SC])
    r.s.activation(sp2, sp2, AF.Ln, bias=c.consts[:SH, 3:4]).inc(SEM_SC)
    r.v.wait(SEM_SC, r.n[SEM_SC])
    r.v.tensor_scalar_max(dtfm, sp1, c.consts[:SH, 4:5]).inc(SEM_VE)
    r.v.wait(SEM_VE, r.n[SEM_VE])
    r.v.tensor_add(dtfm, dtfm, sp2).inc(SEM_VE)
    r.s.activation(expal, c.alb_sb[:, 0:1], AF.Exp).inc(SEM_SC)
    r.v.wait(SEM_SC, r.n[SEM_SC])
    r.v.wait(SEM_VE, r.n[SEM_VE])
    r.v.tensor_scalar_mul(expal, expal, c.consts[:SH, 5:6]).inc(SEM_VE)
    r.v.wait(SEM_VE, r.n[SEM_VE])
    r.v.tensor_scalar_mul(sp1, dtfm, expal[:, 0:1]).inc(SEM_VE)
    if PREP_CUT == "SP":
        return
    r.v.wait(SEM_LA, r.n[SEM_LA])
    r.v.wait(SEM_VE, r.n[SEM_VE])
    r.v.tensor_tensor_scan(acs, smask, sp1, 0.0, ALU.mult, ALU.add).inc(SEM_VE)
    r.s.wait(SEM_VE, r.n[SEM_VE])
    r.s.activation(eacs, acs, AF.Exp).inc(SEM_SC)
    lastv = acs.rearrange("h (t n) -> h t n", n=128)[:, :, 127:128]
    r.v.wait(SEM_SC, r.n[SEM_SC])
    r.v.tensor_sub(sp2.rearrange("h (t n) -> h t n", n=128),
                   lastv.broadcast_to([SH, TT, 128]),
                   acs.rearrange("h (t n) -> h t n", n=128)).inc(SEM_VE)
    r.s.wait(SEM_VE, r.n[SEM_VE])
    r.s.activation(dstf, sp2, AF.Exp).inc(SEM_SC)
    prep_sc = r.n[SEM_SC]
    if PREP_CUT == "SCAN":
        return
    for t in range(TT):
        for i, (src_, dst_) in enumerate(
                [(acs, acst), (eacs, eacst), (dstf, dstt), (dtfm, dtvt)]):
            bk = 2 + (i % 2)
            r.t.wait(SEM_SC, prep_sc)
            r.t.wait(SEM_VE, r.n[SEM_VE])
            r.t.transpose(c.pb[bk][:, 0:32],
                          src_[:, t * 128:(t + 1) * 128],
                          c.identf_sb[:SH, 0:SH]).inc(SEM_PE)
            r.v.wait(SEM_PE, r.n[SEM_PE])
            r.v.tensor_copy(dst_[:, t * 32:(t + 1) * 32],
                            c.pb[bk][:, 0:32]).inc(SEM_VE)
    if PREP_CUT == "TM":
        return
    r.y.wait(SEM_VE, r.n[SEM_VE])
    r.y.dma(c.dAd, eacst[127:128, :], SEM_ST)
    if PREP_CUT == "DA1":
        return
    r.y.wait(SEM_ST, r.n[SEM_ST])
    r.y.dma(c.rowb[0:1, 0:1024], c.dAd, SEM_LA)
    r.y.dma(c.acsd, acs, SEM_ST)
    la = r.n[SEM_LA]
    if PREP_CUT == "DA2":
        return
    r.t.wait(SEM_LA, la)
    for q in range(2):
        if q > 0:
            r.t.wait(SEM_VE, r.n[SEM_VE])
        r.t.matmul(c.pb[6][:, :], c.onesf_sb[:, :],
                   c.rowb[0:1, q * 512:(q + 1) * 512],
                   start=True, stop=True).inc(SEM_PE)
        r.v.wait(SEM_PE, r.n[SEM_PE])
        r.v.tensor_copy(dab[:, q * 512:(q + 1) * 512],
                        c.pb[6][:, :]).inc(SEM_VE)
    c.prep_done = [(SEM_VE, r.n[SEM_VE]), (SEM_SC, r.n[SEM_SC]),
                   (SEM_ST, r.n[SEM_ST]), (SEM_PE, r.n[SEM_PE])]
    c.ssd_tm = (acst, eacst, dstt, dtvt, dab)


def _p_ssd_heads(c):
    r = c.r
    L = c.LNX
    acst, eacst, dstt, dtvt, dab = c.ssd_tm
    Bf = _carve(L, 0, BF16, [64, S])
    Vf = _carve(L, 8192, BF16, [64, S])
    Cf = _carve(L, 16384, BF16, [64, S])
    gf = _carve(L, 24576, BF16, [64, S])
    yst = _carve(L, 16384, BF16, [64, S])
    Btm = _carve(L, 96 * 1024, BF16, [128, NCH * 64])
    Vtm = _carve(L, 100 * 1024, BF16, [128, NCH * 64])
    xtm = _carve(L, 104 * 1024, BF16, [128, NCH * 64])
    Bwt = _carve(L, 108 * 1024, BF16, [128, NCH * 64])
    acsb = _carve(L, 112 * 1024, F32, [128, S])
    sqb = _carve(L, 80 * 1024, BF16, [128, NCH * 64])
    ygb = _carve(L, 84 * 1024, BF16, [128, NCH * 64])
    msb = _carve(L, 88 * 1024, F32, [128, 64])
    LT = _carve(c.W, 0, F32, [128, S])
    gtm = _carve(c.W, 67584, BF16, [128, NCH * 64])
    Mt = [_carve(c.W, 65536 + i * 1024, BF16, [128, 512]) for i in range(2)]
    ydt = _carve(c.sm, 5632, F32, [128, 256])
    carry = [_carve(c.sm, 2432 + i * 256, F32, [64, 64]) for i in range(2)]
    carryb = [_carve(c.sm, 2944 + i * 128, BF16, [64, 64]) for i in range(2)]
    ysb = _carve(c.W, 32768, F32, [128, NCH * 64])

    for g_ in c.prep_done:
        r.y.wait(*g_)
        r.g.wait(*g_)
    kv_free = None
    cf_free = None
    gf_free = None
    row_free = None
    tm_free = None
    acsb_free = None
    LT_free = None
    Mt_free = [None, None]
    bank_free = {}
    ysb_free = None
    ydt_free = None
    yst_free = None
    sq_free = None
    yg_free = None
    gtm_free = None
    for kvh in range(SKVH):
        if kv_free is not None:
            r.y.wait(*kv_free)
        vcb, vhf = kvh // 2, kvh % 2
        r.y.dma(Bf[:, :], c.vkqpost[4 + vcb][vhf * 64:vhf * 64 + 64, :], SEM_LA)
        r.y.dma(Vf[:, :], c.vkqpost[vcb][vhf * 64:vhf * 64 + 64, :], SEM_LA)
        kv_la = r.n[SEM_LA]
        if tm_free is not None:
            r.t.wait(*tm_free)
        r.t.wait(SEM_LA, kv_la)
        for cch in range(NCH):
            bk = cch % 2
            pv = c.pb[bk][:, :].bitcast(BF16)
            if bank_free.get(("tm", bk)) is not None:
                r.t.wait(*bank_free[("tm", bk)])
            r.t.transpose(pv[:, 0:64], Bf[:, cch * 128:(cch + 1) * 128],
                          c.identb_sb[:64, 0:64])
            r.t.transpose(pv[:, 64:128], Vf[:, cch * 128:(cch + 1) * 128],
                          c.identb_sb[:64, 0:64]).inc(SEM_PE)
            r.v.wait(SEM_PE, r.n[SEM_PE])
            r.v.tensor_copy(Btm[:, cch * 64:(cch + 1) * 64], pv[:, 0:64])
            r.v.tensor_copy(Vtm[:, cch * 64:(cch + 1) * 64],
                            pv[:, 64:128]).inc(SEM_VE)
            bank_free[("tm", bk)] = (SEM_VE, r.n[SEM_VE])
        btm_ve = r.n[SEM_VE]
        for hh in range(4):
            h = kvh * 4 + hh
            qcb, qhf = (1024 + h * 64) // 128, (h * 64 // 64) % 2
            if cf_free is not None:
                r.y.wait(*cf_free)
            r.y.dma(Cf[:, :], c.vkqpost[qcb][qhf * 64:qhf * 64 + 64, :], SEM_LA)
            if gf_free is not None:
                r.y.wait(*gf_free)
            gcb, ghf = (h * 64) // 128, h % 2
            r.y.dma(gf[:, :], c.gfm[gcb][ghf * 64:ghf * 64 + 64, :], SEM_LA)
            if row_free is not None:
                r.y.wait(*row_free)
            r.y.dma(c.rowb[0:1, :], c.acsd[h:h + 1, :], SEM_LA)
            h_la = r.n[SEM_LA]
            dtv_v = dtvt.rearrange("p (t j) -> p t j", j=32)[:, :, h:h + 1]
            dst_v = dstt.rearrange("p (t j) -> p t j", j=32)[:, :, h:h + 1]
            eac_v = eacst.rearrange("p (t j) -> p t j", j=32)[:, :, h:h + 1]
            acs_v = acst.rearrange("p (t j) -> p t j", j=32)[:, :, h:h + 1]
            x3 = xtm.rearrange("p (t j) -> p t j", j=64)
            bw3 = Bwt.rearrange("p (t j) -> p t j", j=64)
            bt3 = Btm.rearrange("p (t j) -> p t j", j=64)
            vt3 = Vtm.rearrange("p (t j) -> p t j", j=64)
            r.v.wait(SEM_VE, btm_ve)
            prev_mm = bank_free.get(("xw",))
            if prev_mm is not None:
                r.v.wait(*prev_mm)
            r.v.tensor_mul(x3, vt3, dtv_v.broadcast_to([128, NCH, 64]))
            r.v.tensor_mul(bw3, bt3, dst_v.broadcast_to([128, NCH, 64])).inc(SEM_VE)
            xw_ve = r.n[SEM_VE]
            r.t.wait(SEM_LA, h_la)
            r.t.wait(SEM_VE, xw_ve)
            for q8 in range(8):
                bk = 6 + (q8 % 2)
                if bank_free.get(("b67", bk)) is not None:
                    r.t.wait(*bank_free[("b67", bk)])
                r.t.matmul(c.pb[bk][:, :], c.onesf_sb[:, :],
                           c.rowb[0:1, q8 * 512:(q8 + 1) * 512],
                           start=True, stop=True).inc(SEM_PE)
                if acsb_free is not None and q8 == 0:
                    r.v.wait(*acsb_free)
                r.v.wait(SEM_PE, r.n[SEM_PE])
                r.v.tensor_copy(acsb[:, q8 * 512:(q8 + 1) * 512],
                                c.pb[bk][:, :]).inc(SEM_VE)
                bank_free[("b67", bk)] = (SEM_VE, r.n[SEM_VE])
            row_free = (SEM_PE, r.n[SEM_PE])
            if LT_free is not None:
                r.v.wait(*LT_free)
            r.v.wait(SEM_VE, r.n[SEM_VE])
            r.v.tensor_sub(LT.rearrange("p (t l) -> p t l", l=128),
                           acsb.rearrange("p (t l) -> p t l", l=128),
                           acs_v.broadcast_to([128, NCH, 128])).inc(SEM_VE)
            r.v.wait(SEM_VE, r.n[SEM_VE])
            r.v.tensor_add(LT.rearrange("p (t l) -> p t l", l=128),
                           LT.rearrange("p (t l) -> p t l", l=128),
                           c.segneg_sb[:, :].unsqueeze(1)
                           .broadcast_to([128, NCH, 128])).inc(SEM_VE)
            acsb_free = (SEM_VE, r.n[SEM_VE])
            r.s.wait(SEM_VE, r.n[SEM_VE])
            r.s.activation(LT, LT, AF.Exp).inc(SEM_SC)
            lt_sc = r.n[SEM_SC]
            r.v.memset(carry[0][:, :], 0.0)
            r.v.memset(carryb[0][:, :], 0.0).inc(SEM_VE)
            stt_prev = r.n[SEM_VE]
            for cq in range(8):
                bcb = (cq % 2)
                if bank_free.get(("bcq", bcb)) is not None:
                    r.t.wait(*bank_free[("bcq", bcb)])
                for c4 in range(4):
                    cch = cq * 4 + c4
                    ins = r.t.matmul(
                        c.pb[bcb][:, :].rearrange("p (a b) -> p a b", a=4)[:, c4, :],
                        Bf[:, cch * 128:(cch + 1) * 128],
                        Cf[:, cch * 128:(cch + 1) * 128],
                        start=(c4 == 0), stop=(c4 == 3))
                ins.inc(SEM_PE)
                bcq_pe = r.n[SEM_PE]
                mb = Mt[cq % 2]
                if Mt_free[cq % 2] is not None:
                    r.v.wait(*Mt_free[cq % 2])
                r.v.wait(SEM_PE, bcq_pe)
                r.v.wait(SEM_SC, lt_sc)
                r.v.tensor_mul(mb[:, :], c.pb[bcb][:, :],
                               LT[:, cq * 512:(cq + 1) * 512]).inc(SEM_VE)
                mt_ve = r.n[SEM_VE]
                bank_free[("bcq", bcb)] = (SEM_VE, mt_ve)
                ydb = 2 + (cq % 2)
                ssb = 5
                if bank_free.get(("yd", ydb)) is not None:
                    r.t.wait(*bank_free[("yd", ydb)])
                if bank_free.get(("yo",)) is not None:
                    r.t.wait(*bank_free[("yo",)])
                r.t.wait(SEM_VE, mt_ve)
                for c4 in range(4):
                    cch = cq * 4 + c4
                    r.t.matmul(c.pb[ydb][:, :].rearrange(
                        "p (a b) -> p a b", a=8)[:, c4, :],
                        mb[:, c4 * 128:(c4 + 1) * 128],
                        x3[:, cch, :], start=(c4 == 0), stop=(c4 == 3))
                    r.t.wait(SEM_VE, stt_prev)
                    r.t.matmul(c.pb[ssb][:64, 0:64], bw3[:, cch, :],
                               x3[:, cch, :], start=True, stop=True).inc(SEM_PE)
                    s_pe = r.n[SEM_PE]
                    cin = carry[cch % 2]
                    cout = carry[(cch + 1) % 2]
                    yob = 6 + (c4 % 2)
                    if bank_free.get(("b67", yob)) is not None:
                        r.t.wait(*bank_free[("b67", yob)])
                    r.t.matmul(c.pb[yob][:, 0:64],
                        Cf[:, cch * 128:(cch + 1) * 128],
                        carryb[cch % 2][:, :],
                        start=True, stop=True).inc(SEM_PE)
                    yo_pe = r.n[SEM_PE]
                    yo_pe_of = (yo_pe, yob)
                    r.v.wait(SEM_PE, yo_pe)
                    r.v.scalar_tensor_tensor(
                        cout[:, :], cin[:, :],
                        dab[:64, cch * 32 + h:cch * 32 + h + 1],
                        c.pb[ssb][:64, 0:64], ALU.mult, ALU.add).inc(SEM_VE)
                    r.v.wait(SEM_VE, r.n[SEM_VE])
                    r.v.tensor_copy(carryb[(cch + 1) % 2][:, :],
                                    cout[:, :]).inc(SEM_VE)
                    stt_prev = r.n[SEM_VE]
                    if cq == 0 and c4 == 0 and ysb_free is not None:
                        r.v.wait(*ysb_free)
                    if SSD_DEV == "yo" and h == 0:
                        r.v.tensor_copy(ydt[:, 0:64], c.pb[yob][:, 0:64]).inc(SEM_VE)
                        r.g.wait(SEM_VE, r.n[SEM_VE])
                        r.g.dma(c.yod[:, cch * 64:(cch + 1) * 64],
                                ydt[:, 0:64], SEM_ST)
                        r.g.dma(c.cbd[:, cch * 64:(cch + 1) * 64],
                                carryb[cch % 2][:, :], SEM_ST)
                        r.v.wait(SEM_ST, r.n[SEM_ST])
                    r.v.tensor_scalar_mul(
                        ysb[:, cch * 64:(cch + 1) * 64],
                        c.pb[yob][:, 0:64],
                        eacst[:, cch * 32 + h:cch * 32 + h + 1]).inc(SEM_VE)
                    bank_free[("b67", yob)] = (SEM_VE, r.n[SEM_VE])
                yd_pe_yo = r.n[SEM_PE]
                if ydt_free is not None:
                    r.v.wait(*ydt_free)
                r.v.wait(SEM_PE, yd_pe_yo)
                r.v.tensor_copy(ydt[:, :], c.pb[ydb][:, 0:256]).inc(SEM_VE)
                bank_free[("yd", ydb)] = (SEM_VE, r.n[SEM_VE])
                yv = ysb.rearrange("p (t j) -> p t j", j=64)[:, cq * 4:(cq + 1) * 4, :]
                r.v.wait(SEM_VE, r.n[SEM_VE])
                if SSD_DEV == "split":
                    if h == 0:
                        r.g.wait(SEM_VE, r.n[SEM_VE])
                        r.g.dma(c.ydd[:, cq * 256:(cq + 1) * 256],
                                ydt[:, :], SEM_ST)
                    r.v.nop().inc(SEM_VE)
                else:
                    r.v.tensor_add(yv, yv,
                                   ydt[:, :].rearrange("p (a b) -> p a b", a=4)
                                   ).inc(SEM_VE)
                bank_free[("yo",)] = (SEM_VE, r.n[SEM_VE])
                ydt_free = (SEM_VE, r.n[SEM_VE])
            bank_free[("xw",)] = (SEM_PE, r.n[SEM_PE])
            ysb_ve = r.n[SEM_VE]
            if h == 0:
                r.g.wait(SEM_VE, r.n[SEM_VE])
                r.g.dma(c.ysbd, ysb[:, :], SEM_ST)
                r.g.dma(c.ltd, LT[:, :], SEM_ST)
                r.g.dma(c.xtmd, xtm[:, :], SEM_ST)

            last_mt_ve = r.n[SEM_VE]
            if sq_free is not None:
                r.s.wait(*sq_free)
            r.s.wait(SEM_VE, ysb_ve)
            r.s.activation(sqb[:, :], ysb[:, :], AF.Square).inc(SEM_SC)
            r.v.wait(SEM_SC, r.n[SEM_SC])
            r.v.tensor_reduce(msb[:, 0:32],
                              sqb.rearrange("p (t j) -> p t j", j=64),
                              axis=AX.X, op=ALU.add).inc(SEM_VE)
            r.s.wait(SEM_VE, r.n[SEM_VE])
            r.s.activation(msb[:, 32:64], msb[:, 0:32], AF.Sqrt,
                           bias=c.consts[:, 2:3],
                           scale=1.0 / SP).inc(SEM_SC)
            r.v.wait(SEM_SC, r.n[SEM_SC])
            r.v.reciprocal(msb[:, 0:32], msb[:, 32:64]).inc(SEM_VE)
            r.v.wait(SEM_VE, r.n[SEM_VE])
            r.v.tensor_mul(ysb.rearrange("p (t j) -> p t j", j=64),
                           ysb.rearrange("p (t j) -> p t j", j=64),
                           msb[:, 0:32].unsqueeze(2)
                           .broadcast_to([128, 32, 64])).inc(SEM_VE)
            r.v.wait(SEM_VE, r.n[SEM_VE])
            r.v.tensor_mul(ysb.rearrange("p (t j) -> p t j", j=64),
                           ysb.rearrange("p (t j) -> p t j", j=64),
                           c.gnwb_sb[:, :].unsqueeze(1)
                           .broadcast_to([128, 32, 64])).inc(SEM_VE)
            yn_ve = r.n[SEM_VE]
            for cq in range(8):
                bk = (cq % 2)
                pv = c.pb[bk][:, :].bitcast(BF16)
                if bank_free.get(("gt", bk)) is not None:
                    r.t.wait(*bank_free[("gt", bk)])
                r.t.wait(SEM_VE, last_mt_ve)
                r.t.wait(SEM_LA, h_la)
                for c4 in range(4):
                    cch = cq * 4 + c4
                    ins = r.t.transpose(pv[:, c4 * 64:(c4 + 1) * 64],
                                        gf[:, cch * 128:(cch + 1) * 128],
                                        c.identb_sb[:64, 0:64])
                ins.inc(SEM_PE)
                if gtm_free is not None and cq == 0:
                    r.v.wait(*gtm_free)
                r.v.wait(SEM_PE, r.n[SEM_PE])
                r.v.tensor_copy(gtm[:, cq * 256:(cq + 1) * 256],
                                pv[:, 0:256]).inc(SEM_VE)
                bank_free[("gt", bk)] = (SEM_VE, r.n[SEM_VE])
            gf_free = (SEM_PE, r.n[SEM_PE])
            r.s.wait(SEM_VE, r.n[SEM_VE])
            if sq_free is not None:
                pass
            r.s.activation(sqb[:, :], gtm[:, :], AF.Silu).inc(SEM_SC)
            sq_free = (SEM_SC, r.n[SEM_SC])
            if yg_free is not None:
                r.v.wait(*yg_free)
            r.v.wait(SEM_SC, r.n[SEM_SC])
            r.v.wait(SEM_VE, yn_ve)
            r.v.tensor_mul(ygb[:, :], ysb[:, :], sqb[:, :]).inc(SEM_VE)
            ysb_free = (SEM_VE, r.n[SEM_VE])
            gtm_free = (SEM_VE, r.n[SEM_VE])
            if yst_free is not None:
                r.t.wait(*yst_free)
            for cq in range(8):
                bk = (cq % 2)
                pv = c.pb[bk][:, :].bitcast(BF16)
                if bank_free.get(("yt", bk)) is not None:
                    r.t.wait(*bank_free[("yt", bk)])
                r.t.wait(SEM_VE, r.n[SEM_VE])
                for c4 in range(4):
                    cch = cq * 4 + c4
                    ins = r.t.transpose(pv[:64, c4 * 128:(c4 + 1) * 128],
                                        ygb[:, cch * 64:(cch + 1) * 64],
                                        c.identb_sb)
                ins.inc(SEM_PE)
                r.v.wait(SEM_PE, r.n[SEM_PE])
                r.v.tensor_copy(yst[:64, cq * 512:(cq + 1) * 512],
                                pv[:64, 0:512]).inc(SEM_VE)
                bank_free[("yt", bk)] = (SEM_VE, r.n[SEM_VE])
            yg_free = (SEM_PE, r.n[SEM_PE])
            r.y.wait(SEM_VE, r.n[SEM_VE])
            jb, jhf = h // 2, h % 2
            r.y.dma(c.yfm[jb][jhf * 64:jhf * 64 + 64, :], yst[:64, :], SEM_ST)
            yst_free = (SEM_ST, r.n[SEM_ST])
            cf_free = (SEM_ST, r.n[SEM_ST])
        kv_free = (SEM_PE, r.n[SEM_PE])
    c.ssd_done = [(SEM_PE, r.n[SEM_PE]), (SEM_VE, r.n[SEM_VE]),
                  (SEM_ST, r.n[SEM_ST]), (SEM_SC, r.n[SEM_SC])]


def _p_debug(c):
    r = c.r
    for name, out in c.dbg.items():
        for sem in (SEM_VE, SEM_SC, SEM_PE, SEM_GP, SEM_AD, SEM_CC, SEM_ST):
            r.g.wait(sem, r.n[sem])
        if name == "LNX":
            r.g.dma(out, c.LNX[:, :], SEM_ST)
        else:
            src = getattr(c, name)
            if src.dtype == F32:
                r.g.dma(out, src, SEM_ST)
            else:
                r.g.dma(out, src, SEM_ST)


def _p_final(c):
    r = c.r
    r.y.wait(SEM_ST, r.n[SEM_ST])


def kernel(**inputs):
    nc = build_program()
    maps = make_in_maps(inputs)
    res = run_bass_kernel_spmd(nc, maps, core_ids=list(range(8)))
    out = np.stack([res.results[2 * b]["y"] for b in range(4)], axis=0)
    return out


# version 02
# speedup vs baseline: 1.1191x; 1.1191x over previous
import contextlib
import numpy as np
import ml_dtypes

import concourse.bass as bass
import concourse.mybir as mybir
from concourse.bass_utils import run_bass_kernel_spmd

F32 = mybir.dt.float32
BF16 = mybir.dt.bfloat16
AF = mybir.ActivationFunctionType
ALU = mybir.AluOpType
AX = mybir.AxisListType

S = 4096
D = 2048
TT = 32
DC = 16
NEG = -1e30
SHIFT = 20.0
LN_EPS = 1e-5
GN_EPS = 1e-5

AH, AHD = 16, 128
ASCALE = 1.0 / np.sqrt(AHD)
NE = 16
NPASS = 10
IB = 11
I_MOE = 1408
I_SHH = 2816
SH, SKVH, SP = 32, 8, 64
CONV = 3072
PROJ = 5152
NCH = 32

RG_PAIRS = [[0, 1], [2, 3], [4, 5], [6, 7]]


ENGS = ("sync", "tensor", "vector", "scalar", "gpsimd")


class _Op:
    __slots__ = ("eng", "meth", "args", "kw", "incs")

    def __init__(self, eng, meth, args, kw):
        self.eng, self.meth, self.args, self.kw = eng, meth, args, kw
        self.incs = []


class _OpRef:
    __slots__ = ("rec", "op")

    def __init__(self, rec, op):
        self.rec, self.op = rec, op

    def inc(self, sem, n=1):
        self.op.incs.append((sem, n))
        self.rec.n[sem] += n
        return self


class _EngRec:
    def __init__(self, rec, eng):
        self._rec, self._eng = rec, eng

    def __getattr__(self, meth):
        def call(*args, **kw):
            op = _Op(self._eng, meth, args, kw)
            self._rec.ops.append(op)
            return _OpRef(self._rec, op)
        return call

    def wait(self, sem, val=None):
        if val is None:
            val = self._rec.n[sem]
        return self.__getattr__("wait_ge")(sem, val)

    def dma(self, out, in_, sem, **kw):
        return self.__getattr__("dma_start")(out, in_, **kw).inc(sem, 16)


class Rec:
    def __init__(self, semnames):
        self.ops = []
        self.n = {s: 0 for s in semnames}
        self.t = _EngRec(self, "tensor")
        self.v = _EngRec(self, "vector")
        self.s = _EngRec(self, "scalar")
        self.g = _EngRec(self, "gpsimd")
        self.y = _EngRec(self, "sync")

    def emit(self, block, sems):
        def mk(ename):
            def f(e):
                for op in self.ops:
                    if op.eng != ename:
                        continue
                    args = [sems[a] if isinstance(a, _SemName) else a
                            for a in op.args]
                    ins = getattr(e, op.meth)(*args, **op.kw)
                    for sem, k in op.incs:
                        ins.then_inc(sems[sem], k)
            return f
        block.sync(mk("sync"))
        block.tensor(mk("tensor"))
        block.vector(mk("vector"))
        block.scalar(mk("scalar"))
        block.gpsimd(mk("gpsimd"))


class _SemName(str):
    pass


SEMS = [_SemName(x) for x in
        ("LW", "LA", "LB", "ST", "PE", "VE", "SC", "GP", "CC", "AD")]
(SEM_LW, SEM_LA, SEM_LB, SEM_ST, SEM_PE, SEM_VE, SEM_SC, SEM_GP, SEM_CC,
 SEM_AD) = SEMS


def _rope_tables():
    freqs = 1.0 / 10000.0 ** (np.arange(0, AHD, 2, dtype=np.float32) / AHD)
    pos = np.arange(S, dtype=np.float32)
    ang = np.outer(pos, freqs)
    return np.cos(ang).astype(np.float32), np.sin(ang).astype(np.float32)


def _rope_perm():
    p = np.empty(D, dtype=np.int64)
    for h in range(AH):
        base = h * AHD
        p[base:base + 64] = base + 2 * np.arange(64)
        p[base + 64:base + 128] = base + 2 * np.arange(64) + 1
    return p


def _tile_w(w):
    return np.ascontiguousarray(w.reshape(DC, 128, -1))


def _consts():
    c = {}
    c["identb"] = np.eye(128, dtype=ml_dtypes.bfloat16)
    c["identf"] = np.eye(128, dtype=np.float32)
    c["onesb"] = np.ones((1, 128), dtype=ml_dtypes.bfloat16)
    c["onesf"] = np.ones((1, 128), dtype=np.float32)
    cos, sin = _rope_tables()
    c["ropec"], c["ropes"] = cos, sin
    m = np.zeros((4, 128, 512), dtype=np.float32)
    for j in range(4):
        for qs in range(4):
            q_global = qs * 128 + np.arange(128)[None, :]
            k_global = j * 128 + np.arange(128)[:, None]
            m[j][:, qs * 128:(qs + 1) * 128] = np.where(
                k_global > q_global, NEG, 0.0)
    c["maskT"] = m.reshape(4, 128, 512)
    sidx = np.arange(128)[:, None]
    lidx = np.arange(128)[None, :]
    c["segneg"] = np.where(lidx >= sidx, 0.0, -1e38).astype(np.float32)
    sm = np.ones(S, dtype=np.float32)
    sm[0::128] = 0.0
    c["scanmask"] = np.broadcast_to(sm, (SH, S)).copy()
    return c


def make_in_maps(inp):
    perm = _rope_perm()
    cst = _consts()
    wq = np.ascontiguousarray(_tile_w(inp["L0_attn_a_Wq"])[:, :, perm])
    wk = np.ascontiguousarray(_tile_w(inp["L0_attn_a_Wk"])[:, :, perm])
    wv = _tile_w(inp["L0_attn_a_Wv"])
    wo = _tile_w(inp["L0_attn_a_Wo"])
    wproj = _tile_w(np.ascontiguousarray(inp["L0_attn_b_W_vkqgdt"].T))
    wo_ssd = _tile_w(np.ascontiguousarray(inp["L0_attn_b_W_o"].T))
    convp = np.ascontiguousarray(
        inp["L0_attn_b_conv_w"].reshape(24, 128, 2).transpose(1, 0, 2).reshape(128, 48))
    convb = np.ascontiguousarray(
        inp["L0_attn_b_conv_b"].reshape(24, 128).T)
    dtb = inp["L0_attn_b_dt_bias"].reshape(SH, 1)
    alb = inp["L0_attn_b_A_log_bias"].reshape(SH, 1)
    gnw = inp["L0_attn_b_gnorm_w"].reshape(1, SP)
    rw = inp["L0_ffn_router_w"]
    shgw = inp["L0_ffn_sh_gate_w"]
    wg_all, wu_all, wd_all = (inp["L0_ffn_w_gate"], inp["L0_ffn_w_up"],
                              inp["L0_ffn_w_down"])
    shg, shu, shd = (inp["L0_ffn_sh_gate"], inp["L0_ffn_sh_up"],
                     inp["L0_ffn_sh_down"])
    x = inp["L0_attn_a_x"]

    maps = []
    for c in range(8):
        b, s = c // 2, c % 2
        own = list(range(s * 8, s * 8 + 8))
        other = list(range((1 - s) * 8, (1 - s) * 8 + 8))
        rt17 = np.concatenate(
            [rw[own].T, rw[other].T, shgw], axis=1)
        wg = np.empty((NPASS, IB, DC, 128, 128), dtype=np.float32)
        wu = np.empty_like(wg)
        wd = np.empty((NPASS, IB, 128, D), dtype=np.float32)
        for p in range(8):
            e = own[p]
            wg[p] = wg_all[e].reshape(DC, 128, IB, 128).transpose(2, 0, 1, 3)
            wu[p] = wu_all[e].reshape(DC, 128, IB, 128).transpose(2, 0, 1, 3)
            wd[p] = wd_all[e].reshape(IB, 128, D)
        for p in (8, 9):
            lo = s * I_SHH + (p - 8) * I_MOE
            wg[p] = shg[:, lo:lo + I_MOE].reshape(DC, 128, IB, 128).transpose(2, 0, 1, 3)
            wu[p] = shu[:, lo:lo + I_MOE].reshape(DC, 128, IB, 128).transpose(2, 0, 1, 3)
            wd[p] = shd[lo:lo + I_MOE].reshape(IB, 128, D)
        m = {
            "x0": np.ascontiguousarray(x[b]),
            "wq": wq, "wk": wk, "wv": wv, "wo": wo,
            "rt17": np.ascontiguousarray(_tile_w(rt17)),
            "wg": np.ascontiguousarray(wg), "wu": np.ascontiguousarray(wu),
            "wd": np.ascontiguousarray(wd),
            "wproj": wproj, "wo_ssd": wo_ssd,
            "convp": convp, "convb": convb, "dtb": dtb, "alb": alb, "gnw": gnw,
        }
        m.update(cst)
        maps.append(m)
    return maps


class Ctx:
    pass


def _carve(arena, byte_off, dtype, shape):
    esz = {F32: 4, BF16: 2}[dtype]
    nel = 1
    for d in shape[1:]:
        nel *= d
    a = arena[:shape[0], byte_off // 2: byte_off // 2 + (nel * esz) // 2]
    v = a.bitcast(dtype)
    if len(shape) == 2:
        return v
    spec = "p (" + " ".join(f"d{i}" for i in range(1, len(shape))) + ") -> p " + \
        " ".join(f"d{i}" for i in range(1, len(shape)))
    kw = {f"d{i}": shape[i] for i in range(1, len(shape))}
    return v.rearrange(spec, **kw)


def build_program(upto=None, debug=()):
    nc = bass.Bass("TRN2", target_bir_lowering=False, debug=False)
    c = Ctx()
    c.nc = nc

    def din(name, shape, dt=F32):
        return nc.dram_tensor(name, shape, dt, kind="ExternalInput").ap()

    c.x0 = din("x0", [S, D])
    c.wq, c.wk, c.wv, c.wo = (din(n, [DC, 128, D]) for n in ("wq", "wk", "wv", "wo"))
    c.rt17 = din("rt17", [DC, 128, 17])
    c.wg = din("wg", [NPASS, IB, DC, 128, 128])
    c.wu = din("wu", [NPASS, IB, DC, 128, 128])
    c.wd = din("wd", [NPASS, IB, 128, D])
    c.wproj = din("wproj", [DC, 128, PROJ])
    c.wo_ssd = din("wo_ssd", [DC, 128, D])
    c.convp = din("convp", [128, 48])
    c.convb = din("convb", [128, 24])
    c.dtb = din("dtb", [SH, 1])
    c.alb = din("alb", [SH, 1])
    c.gnw = din("gnw", [1, SP])
    c.identb = din("identb", [128, 128], BF16)
    c.identf = din("identf", [128, 128])
    c.onesb = din("onesb", [1, 128], BF16)
    c.onesf = din("onesf", [1, 128])
    c.ropec = din("ropec", [S, 64])
    c.ropes = din("ropes", [S, 64])
    c.maskT = din("maskT", [4, 128, 512])
    c.segneg = din("segneg", [128, 128])
    c.scanmask = din("scanmask", [SH, S])

    c.y = nc.dram_tensor("y", [S, D], F32, kind="ExternalOutput").ap()

    def scratch(name, shape, dt):
        return nc.dram_tensor(name, shape, dt).ap()
    c.xA = scratch("xA", [S, D], F32)
    c.xB = scratch("xB", [S, D], F32)
    c.qfm = scratch("qfm", [AH, 128, S], BF16)
    c.kfm = scratch("kfm", [AH, 128, S], BF16)
    c.vtm = scratch("vtm", [S, D], BF16)
    c.cfm = scratch("cfm", [AH, 128, S], BF16)
    c.act = scratch("act", [NPASS, IB, 128, S], BF16)
    c.part = scratch("part", [S, D], F32)
    c.red = scratch("red", [S, D], F32)
    c.combd = scratch("combd", [17, S], F32)
    c.vkqpre = scratch("vkqpre", [24, 128, S], BF16)
    c.vkqpost = scratch("vkqpost", [24, 128, S], BF16)
    c.gfm = scratch("gfm", [16, 128, S], BF16)
    c.yfm = scratch("yfm", [16, 128, S], BF16)
    c.acsd = scratch("acsd", [SH, S], F32)
    c.ysbd = scratch("ysbd", [128, NCH * 64], F32)
    c.ltd = scratch("ltd", [128, S], F32)
    c.xtmd = scratch("xtmd", [128, NCH * 64], F32)
    c.ydd = scratch("ydd", [128, NCH * 64], F32)
    c.yod = scratch("yod", [128, NCH * 64], F32)
    c.cbd = scratch("cbd", [64, NCH * 64], F32)
    c.dAd = scratch("dAd", [1, 1024], F32)
    dbg = {}
    for name in debug:
        if name == "LNX":
            dbg[name] = nc.dram_tensor("dbg_LNX", [128, DC * S], F32,
                                       kind="ExternalOutput").ap()
        else:
            src = getattr(c, name)
            dbg[name] = nc.dram_tensor("dbg_" + name, list(src.shape),
                                       F32, kind="ExternalOutput").ap()
    c.dbg = dbg

    rec = Rec(SEMS)
    c.r = rec

    with contextlib.ExitStack() as st:
        def sb(name, shape, dt):
            return st.enter_context(nc.sbuf_tensor(name, shape, dt))

        c.LNX = sb("LNX", [128, DC * S], BF16)
        c.W = sb("W", [128, 35840], BF16)
        c.sm = sb("sm", [128, 4096], BF16)
        c.combsg = _carve(c.W, 55296, F32, [17, S])
        c.dttm = _carve(c.W, 40960, F32, [128, 1024])
        c.rowb = _carve(c.W, 16384, F32, [1, S])
        pb01 = st.enter_context(nc.psum_tensor("pb01", [128, 1024], F32))
        pb23 = st.enter_context(nc.psum_tensor("pb23", [128, 1024], F32))
        c.pb01, c.pb23 = pb01, pb23
        c.pb = [pb01[:, 0:512], pb01[:, 512:1024],
                pb23[:, 0:512], pb23[:, 512:1024]] + [
            st.enter_context(nc.psum_tensor(f"pb{i}", [128, 512], F32))[:, :]
            for i in range(4, 8)]
        sems = {s: st.enter_context(nc.semaphore(str(s))) for s in SEMS}

        _build_phases(c, upto)

        with nc.Block() as block:
            rec.emit(block, sems)
    return nc


def _sm(c, off, dtype, shape):
    return _carve(c.sm, off, dtype, shape)


def _build_phases(c, upto):
    _p_init(c)
    def attn_wores():
        c.wores_guards = [(SEM_PE, c.attn_done_pe), (SEM_VE, c.attn_done_ve),
                          (SEM_ST, c.attn_done_st)]
        _p_wores(c, c.cfm, c.wo, c.x0, c.xA)

    def ssd_wores():
        c.wores_guards = list(c.ssd_done)
        _p_wores(c, c.yfm, c.wo_ssd, c.xB, c.xA)
    seq = [
        ("LN0", lambda: _p_lnres(c, src=c.x0, add=None, dst=None, ln=True)),
        ("QKV", lambda: _p_attn_qkv(c)),
        ("ATTN", lambda: _p_attn(c)),
        ("WORES_A", attn_wores),
        ("LN1", lambda: _p_lnres(c, src=c.xA, add=None, dst=None, ln=True,
                                 w_guard=c.wores_done[2],
                                 lnx_guard=c.wores_done[:1])),
        ("ROUTER", lambda: _p_router(c)),
        ("F1", lambda: _p_ffn_f1(c)),
        ("F2", lambda: _p_ffn_f2(c)),
        ("AR", lambda: _p_ar(c)),
        ("RES2", lambda: _p_lnres(
            c, src=c.xA, add=c.red, dst=c.xB, ln=True,
            add_gate=lambda t: (SEM_CC, c.ar_cc[t // 8]),
            w_guard=c.f2_done, lnx_guard=[c.f2_done])),
        ("SPROJ", lambda: _p_ssd_proj(c)),
        ("SCONV", lambda: _p_ssd_conv(c)),
        ("SPREP", lambda: _p_ssd_prep(c)),
        ("SHEADS", lambda: _p_ssd_heads(c)),
        ("WORES_S", ssd_wores),
        ("LN3", lambda: _p_lnres(c, src=c.xA, add=None, dst=None, ln=True,
                                 w_guard=c.wores_done[2],
                                 lnx_guard=c.wores_done[:1])),
        ("ROUTER2", lambda: _p_router(c)),
        ("F1B", lambda: _p_ffn_f1(c)),
        ("F2B", lambda: _p_ffn_f2(c)),
        ("AR2", lambda: _p_ar(c)),
        ("OUT", lambda: _p_lnres(
            c, src=c.xA, add=c.red, dst=c.y, ln=False,
            add_gate=lambda t: (SEM_CC, c.ar_cc[t // 8]),
            w_guard=c.f2_done)),
    ]
    for name, fn in seq:
        fn()
        if upto == name:
            break
    _p_debug(c)
    _p_final(c)


def _p_init(c):
    r = c.r
    c.identb_sb = _carve(c.sm, 0, BF16, [128, 128])
    c.onesb_sb = _carve(c.sm, 256, BF16, [1, 128])
    c.identf_sb = _carve(c.sm, 512, F32, [128, 128])
    c.onesf_sb = _carve(c.sm, 1024, F32, [1, 128])
    c.consts = _carve(c.sm, 1536, F32, [128, 16])
    c.convp_sb = _carve(c.sm, 1600, F32, [128, 48])
    c.convb_sb = _carve(c.sm, 1792, F32, [128, 24])
    c.gnwb_sb = _carve(c.sm, 1920, F32, [128, 64])
    c.dtb_sb = _carve(c.sm, 2176, F32, [SH, 1])
    c.alb_sb = _carve(c.sm, 2240, F32, [SH, 1])
    c.gnw_row = _carve(c.sm, 4352, F32, [1, 64])
    c.segneg_sb = _carve(c.sm, 4608, F32, [128, 128])
    r.g.dma(c.identb_sb, c.identb, SEM_LW)
    r.g.dma(c.onesb_sb, c.onesb, SEM_LW)
    r.g.dma(c.identf_sb, c.identf, SEM_LW)
    r.g.dma(c.onesf_sb, c.onesf, SEM_LW)
    r.g.dma(c.convp_sb, c.convp, SEM_LW)
    r.g.dma(c.convb_sb, c.convb, SEM_LW)
    r.g.dma(c.dtb_sb, c.dtb, SEM_LW)
    r.g.dma(c.alb_sb, c.alb, SEM_LW)
    r.g.dma(c.gnw_row, c.gnw, SEM_LW)
    r.g.dma(c.segneg_sb, c.segneg, SEM_LW)
    r.t.wait(SEM_LW, r.n[SEM_LW])
    r.t.matmul(c.pb[0][:, 0:64], c.onesf_sb[:, :], c.gnw_row[0:1, :],
               start=True, stop=True).inc(SEM_PE)
    r.v.wait(SEM_PE, r.n[SEM_PE])
    r.v.tensor_copy(c.gnwb_sb, c.pb[0][:, 0:64]).inc(SEM_VE)
    r.v.memset(c.consts[:, 0:1], LN_EPS)
    r.v.memset(c.consts[:, 1:2], -1e30)
    r.v.memset(c.consts[:, 2:3], GN_EPS)
    r.v.memset(c.consts[:, 3:4], 1.0)
    r.v.memset(c.consts[:, 4:5], 0.0)
    r.v.memset(c.consts[:, 5:6], -1.0)
    r.v.memset(c.consts[:, 6:7], -SHIFT).inc(SEM_VE)
    c.n_init_ve = r.n[SEM_VE]
    c.n_init_lw = r.n[SEM_LW]


def _p_lnres(c, src, add, dst, ln=True, w_off=0, add_gate=None, w_guard=None,
             lnx_guard=None):
    r = c.r
    xin = [_carve(c.W, w_off + i * 8192, F32, [128, D]) for i in range(2)]
    ain = [_carve(c.W, w_off + 16384 + i * 8192, F32, [128, D]) for i in range(2)]
    lno = [_carve(c.W, w_off + 32768 + i * 4096, BF16, [128, D]) for i in range(2)]
    stats = _carve(c.W, w_off + 45056, F32, [128, 16])
    ssum, sumsq, mean, var, rstd, nmr, mm2 = (stats[:, i:i + 1] for i in range(7))
    sq_scr = _carve(c.W, w_off + 46080, BF16, [128, D])

    xv = src.rearrange("(t p) d -> t p d", p=128)
    av = add.rearrange("(t p) d -> t p d", p=128) if add is not None else None
    dv = dst.rearrange("(t p) d -> t p d", p=128) if dst is not None else None

    base_la = r.n[SEM_LA]
    base_lb = r.n[SEM_LB]
    xin_free = [None, None]
    lno_free = [None, None]
    bank_free = {6: None, 7: None}

    for t in range(TT):
        bi = t % 2
        if t < 2 and w_guard is not None:
            r.y.wait(*w_guard)
        if xin_free[bi] is not None:
            sem, val = xin_free[bi]
            r.y.wait(sem, val)
        r.y.dma(xin[bi], xv[t], SEM_LA)
        if add is not None:
            if add_gate is not None:
                g_ = add_gate(t)
                if g_ is not None:
                    r.y.wait(*g_)
            r.y.dma(ain[bi], av[t], SEM_LB)

        r.v.wait(SEM_LA, base_la + 16 * (t + 1))
        if add is not None:
            r.v.wait(SEM_LB, base_lb + 16 * (t + 1))
            r.v.tensor_add(xin[bi][:, :], xin[bi][:, :], ain[bi][:, :]).inc(SEM_VE)
        xn = xin[bi]
        xn_ready = (SEM_VE, r.n[SEM_VE]) if add is not None \
            else (SEM_LA, base_la + 16 * (t + 1))

        if dst is not None:
            if add is not None:
                r.y.wait(*xn_ready)
            r.y.dma(dv[t], xn[:, :], SEM_ST)
            xin_free[bi] = (SEM_ST, r.n[SEM_ST])
        if not ln:
            if dst is None:
                xin_free[bi] = xn_ready
            continue

        mean, negm, mm2n, var2, sd0 = (stats[:, i:i + 1] for i in range(7, 12))
        r.v.reduce_sum(ssum, xn[:, :], axis=AX.X).inc(SEM_VE)
        r.s.wait(*xn_ready)
        r.s.activation(sq_scr[:, :], xn[:, :], AF.Square,
                       accum_out=sumsq).inc(SEM_SC)
        r.s.wait(SEM_SC, r.n[SEM_SC])
        r.s.wait(SEM_VE, r.n[SEM_VE])
        r.s.mul(mean, ssum, 1.0 / D)
        r.s.mul(var, sumsq, 1.0 / D).inc(SEM_SC)
        r.s.wait(SEM_SC, r.n[SEM_SC])
        r.s.mul(negm, mean, -1.0).inc(SEM_SC)
        r.s.wait(SEM_SC, r.n[SEM_SC])
        r.s.mul(mm2n, negm, mean).inc(SEM_SC)
        r.s.wait(SEM_SC, r.n[SEM_SC])
        r.s.add(var2, var, mm2n).inc(SEM_SC)
        r.s.wait(SEM_SC, r.n[SEM_SC])
        r.s.activation(sd0, var2, AF.Sqrt, bias=c.consts[:, 0:1]).inc(SEM_SC)
        r.v.wait(SEM_SC, r.n[SEM_SC])
        r.v.reciprocal(rstd, sd0).inc(SEM_VE)
        r.s.wait(SEM_VE, r.n[SEM_VE])
        r.s.mul(nmr, negm, rstd).inc(SEM_SC)
        r.s.wait(SEM_SC, r.n[SEM_SC])
        if lno_free[bi] is not None:
            r.s.wait(*lno_free[bi])
        r.s.activation(lno[bi][:, :], xn[:, :], AF.Identity,
                       bias=nmr, scale=rstd).inc(SEM_SC)
        xin_free[bi] = (SEM_SC, r.n[SEM_SC])
        lnx_done = r.n[SEM_SC]
        for g4 in range(4):
            bank = 6 + (g4 % 2)
            pv = c.pb[bank][:, :].bitcast(BF16)[:, :512]
            if bank_free[bank] is not None:
                r.t.wait(*bank_free[bank])
            if t == 0 and g4 == 0 and lnx_guard is not None:
                for w_ in lnx_guard:
                    r.v.wait(*w_)
            r.t.wait(SEM_SC, lnx_done)
            for k4 in range(4):
                dc = g4 * 4 + k4
                ins = r.t.transpose(
                    pv.rearrange("p (a b) -> p a b", a=4)[:, k4, :],
                    lno[bi][:, dc * 128:(dc + 1) * 128],
                    c.identb_sb)
            ins.inc(SEM_PE)
            r.v.wait(SEM_PE, r.n[SEM_PE])
            r.v.tensor_copy(_lnx_view(c, g4, t),
                            pv.rearrange("p (a b) -> p a b", a=4)).inc(SEM_VE)
            bank_free[bank] = (SEM_VE, r.n[SEM_VE])
        lno_free[bi] = (SEM_PE, r.n[SEM_PE])
    c.lnres_done = [(SEM_SC, r.n[SEM_SC]), (SEM_PE, r.n[SEM_PE]),
                    (SEM_VE, r.n[SEM_VE]), (SEM_ST, r.n[SEM_ST]),
                    (SEM_LA, r.n[SEM_LA]), (SEM_LB, r.n[SEM_LB])]


def _lnx_view(c, g4, t):
    return c.LNX[:, :].rearrange("p (dc s) -> p dc s", dc=DC)[
        :, g4 * 4:(g4 + 1) * 4, t * 128:(t + 1) * 128]


def _p_router(c):
    r = c.r
    rtw = _carve(c.W, 0, BF16, [128, DC * 17])
    rtv = rtw.rearrange("p (dc j) -> p dc j", dc=DC)
    ew = _carve(c.W, 12288, F32, [128, 16])
    tw = _carve(c.W, 13312, F32, [128, 16])
    kw = _carve(c.W, 14336, F32, [128, 16])
    eq = _carve(c.W, 15360, F32, [128, 16])
    cb = _carve(c.W, 16384, F32, [128, 17])
    sc1 = _carve(c.W, 17472, F32, [128, 8])
    mk, sum4, rc4 = sc1[:, 0:1], sc1[:, 1:2], sc1[:, 2:3]
    lnxv = c.LNX[:, :].rearrange("p (dc s) -> p dc s", dc=DC)

    for w_ in c.lnres_done:
        r.g.wait(*w_)
    r.g.dma(rtv, c.rt17.rearrange("a b c -> b a c"), SEM_LW)
    lw_rt = r.n[SEM_LW]
    bank_free = {0: None, 1: None, 2: None}
    for t in range(TT):
        bk = t % 2
        ps = c.pb[bk]
        if t == 0:
            r.t.wait(SEM_LW, lw_rt)
        if bank_free[bk] is not None:
            r.t.wait(*bank_free[bk])
        for dc in range(DC):
            ins = r.t.matmul(ps[:, 0:17], lnxv[:, dc, t * 128:(t + 1) * 128],
                             rtv[:, dc, :], start=(dc == 0), stop=(dc == DC - 1))
        ins.inc(SEM_PE)
        r.s.wait(SEM_PE, r.n[SEM_PE])
        r.s.wait(SEM_VE, r.n[SEM_VE])
        r.s.activation(ew, ps[:, 0:16], AF.Exp)
        r.s.activation(cb[:, 16:17], ps[:, 16:17], AF.Sigmoid).inc(SEM_SC)
        bank_free[bk] = (SEM_SC, r.n[SEM_SC])
        r.v.wait(SEM_SC, r.n[SEM_SC])
        r.v.tensor_copy(tw, ew).inc(SEM_VE)
        for k in range(4):
            r.v.wait(SEM_VE, r.n[SEM_VE])
            r.v.reduce_max(mk, tw, axis=AX.X).inc(SEM_VE)
            r.v.wait(SEM_VE, r.n[SEM_VE])
            if k == 0:
                r.v.tensor_copy(sum4, mk)
                r.v.tensor_single_scalar(kw, tw, mk, ALU.is_equal).inc(SEM_VE)
                r.v.wait(SEM_VE, r.n[SEM_VE])
                r.v.scalar_tensor_tensor(tw, kw, c.consts[:, 1:2], tw,
                                         ALU.mult, ALU.add).inc(SEM_VE)
            else:
                r.v.tensor_add(sum4, sum4, mk)
                r.v.tensor_single_scalar(eq, tw, mk, ALU.is_equal).inc(SEM_VE)
                r.v.wait(SEM_VE, r.n[SEM_VE])
                r.v.tensor_add(kw, kw, eq)
                r.v.scalar_tensor_tensor(tw, eq, c.consts[:, 1:2], tw,
                                         ALU.mult, ALU.add).inc(SEM_VE)
        r.v.wait(SEM_VE, r.n[SEM_VE])
        r.v.reciprocal(rc4, sum4)
        r.v.tensor_mul(cb[:, 0:16], ew, kw).inc(SEM_VE)
        r.v.wait(SEM_VE, r.n[SEM_VE])
        r.v.tensor_scalar_mul(cb[:, 0:16], cb[:, 0:16], rc4).inc(SEM_VE)
        r.t.wait(SEM_VE, r.n[SEM_VE])
        if bank_free[2] is not None:
            r.t.wait(*bank_free[2])
        r.t.transpose(c.pb[2][:17, :128], cb, c.identf_sb).inc(SEM_PE)
        r.v.wait(SEM_PE, r.n[SEM_PE])
        r.v.tensor_copy(c.combsg[:, t * 128:(t + 1) * 128],
                        c.pb[2][:17, :128]).inc(SEM_VE)
        bank_free[2] = (SEM_VE, r.n[SEM_VE])
    r.y.wait(SEM_VE, r.n[SEM_VE])
    r.y.dma(c.combd, c.combsg[:, :], SEM_ST)
    c.router_done = (SEM_ST, r.n[SEM_ST])


def _p_ffn_f1(c):
    r = c.r
    wgt = [_carve(c.W, 0 + i * 4096, BF16, [128, DC * 128]) for i in range(2)]
    wut = [_carve(c.W, 8192 + i * 4096, BF16, [128, DC * 128]) for i in range(2)]
    scale_sb = _carve(c.W, 16384, BF16, [128, S])
    sgt = [_carve(c.W, 24576 + i * 2048, F32, [128, 512]) for i in range(2)]
    gst = [_carve(c.W, 28672 + i * 2048, F32, [128, 512]) for i in range(2)]
    astg = [_carve(c.W, 32768 + i * 8192, BF16, [128, S]) for i in range(2)]
    lnxv = c.LNX[:, :].rearrange("p (dc s) -> p dc s", dc=DC)

    def load_w(p, ib):
        wb = ib % 2
        if wbuf_free[wb] is not None:
            r.g.wait(*wbuf_free[wb])
        r.g.dma(wgt[wb].rearrange("p (dc j) -> p dc j", dc=DC),
                c.wg[p, ib].rearrange("a b c -> b a c"), SEM_LW)
        r.g.dma(wut[wb].rearrange("p (dc j) -> p dc j", dc=DC),
                c.wu[p, ib].rearrange("a b c -> b a c"), SEM_LW)
        return r.n[SEM_LW]

    wbuf_free = [c.router_done, c.router_done]
    astg_free = [None, None]
    sgt_free = [None, None]
    gst_free = [None, None]
    gate_bank_free = None
    up_bank_free = None
    bcast_bank_free = None
    scale_free = None
    crow_free = None
    c.act_st = {}
    lw_of_ib = {}
    crow = _carve(c.W, 49152, F32, [1, S])
    for p in range(NPASS):
        rs = p if p < 8 else 16
        r.g.wait(*c.router_done)
        if crow_free is not None:
            r.g.wait(*crow_free)
        r.g.dma(crow, c.combd[rs:rs + 1, :], SEM_LB)
        crow_lw = r.n[SEM_LB]
        if up_bank_free is not None:
            r.t.wait(*up_bank_free)
        r.t.wait(SEM_LB, crow_lw)
        for q in range(8):
            if bcast_bank_free is not None:
                r.t.wait(*bcast_bank_free)
            r.t.matmul(c.pb[7][:, :], c.onesf_sb[:, :],
                       crow[0:1, q * 512:(q + 1) * 512],
                       start=True, stop=True).inc(SEM_PE)
            r.v.wait(SEM_PE, r.n[SEM_PE])
            if scale_free is not None:
                r.v.wait(*scale_free)
            r.v.tensor_copy(scale_sb[:, q * 512:(q + 1) * 512],
                            c.pb[7][:, :]).inc(SEM_VE)
            bcast_bank_free = (SEM_VE, r.n[SEM_VE])
        up_bank_free = bcast_bank_free
        crow_free = (SEM_PE, r.n[SEM_PE])
        scale_ready = r.n[SEM_VE]
        if p == 0:
            lw_of_ib[0] = load_w(p, 0)
        for ib in range(IB):
            wb = ib % 2
            lw = lw_of_ib.pop(ib)
            wgv = wgt[wb].rearrange("p (dc j) -> p dc j", dc=DC)
            wuv = wut[wb].rearrange("p (dc j) -> p dc j", dc=DC)
            if ib + 1 < IB:
                lw_of_ib[ib + 1] = load_w(p, ib + 1)
            ab = ib % 2
            for tcg in range(2):
                r.t.wait(SEM_LW, lw)
                if gate_bank_free is not None:
                    r.t.wait(*gate_bank_free)
                for dc in range(DC):
                    for t4 in range(4):
                        tc = tcg * 4 + t4
                        ins = r.t.matmul(c.pb[t4][:, :], wgv[:, dc, :],
                                         lnxv[:, dc, tc * 512:(tc + 1) * 512],
                                         start=(dc == 0), stop=(dc == DC - 1))
                ins.inc(SEM_PE)
                gate_done = r.n[SEM_PE]
                if up_bank_free is not None:
                    r.t.wait(*up_bank_free)
                for dc in range(DC):
                    for t4 in range(4):
                        tc = tcg * 4 + t4
                        ins = r.t.matmul(c.pb[4 + t4][:, :], wuv[:, dc, :],
                                         lnxv[:, dc, tc * 512:(tc + 1) * 512],
                                         start=(dc == 0), stop=(dc == DC - 1))
                ins.inc(SEM_PE)
                up_done = r.n[SEM_PE]
                for t4 in range(4):
                    tc = tcg * 4 + t4
                    eb = t4 % 2
                    if sgt_free[eb] is not None:
                        r.s.wait(*sgt_free[eb])
                    r.s.wait(SEM_PE, gate_done)
                    r.s.activation(sgt[eb][:, :], c.pb[t4][:, :],
                                   AF.Silu).inc(SEM_SC)
                    if gst_free[eb] is not None:
                        r.g.wait(*gst_free[eb])
                    r.g.wait(SEM_VE, scale_ready)
                    r.g.wait(SEM_SC, r.n[SEM_SC])
                    r.g.tensor_mul(gst[eb][:, :], sgt[eb][:, :],
                                   scale_sb[:, tc * 512:(tc + 1) * 512]).inc(SEM_GP)
                    sgt_free[eb] = (SEM_GP, r.n[SEM_GP])
                    if tcg == 0 and t4 == 0 and astg_free[ab] is not None:
                        r.v.wait(*astg_free[ab])
                    r.v.wait(SEM_GP, r.n[SEM_GP])
                    r.v.wait(SEM_PE, up_done)
                    r.v.tensor_mul(astg[ab][:, tc * 512:(tc + 1) * 512],
                                   gst[eb][:, :], c.pb[4 + t4][:, :]).inc(SEM_VE)
                    gst_free[eb] = (SEM_VE, r.n[SEM_VE])
                gate_bank_free = (SEM_SC, r.n[SEM_SC])
                up_bank_free = (SEM_VE, r.n[SEM_VE])
            scale_free = (SEM_GP, r.n[SEM_GP])
            r.y.wait(SEM_VE, r.n[SEM_VE])
            r.y.dma(c.act[p, ib], astg[ab][:, :], SEM_ST)
            astg_free[ab] = (SEM_ST, r.n[SEM_ST])
            wbuf_free[wb] = (SEM_PE, up_done)
        if p + 1 < NPASS:
            lw_of_ib[0] = load_w(p + 1, 0)
        c.act_st[p] = r.n[SEM_ST]
    c.f1_done_pe = r.n[SEM_PE]
    c.f1_done_ve = r.n[SEM_VE]


def _p_ffn_f2(c):
    r = c.r
    wdt = [[_carve(c.LNX, (e2 * IB + ib) * 4096, BF16, [128, D])
            for ib in range(IB)] for e2 in range(2)]
    actt = [[_carve(c.W, (e2 * IB + ib) * 2048, BF16, [128, 1024])
             for ib in range(IB)] for e2 in range(2)]
    stg = [_carve(c.W, 45056, F32, [128, 2048]),
           _carve(c.LNX, 90112, F32, [128, 2048])]
    partv = c.part.rearrange("(t p) d -> t p d", p=128)

    part_seq = {}
    wd_busy_until = (SEM_PE, c.f1_done_pe)
    g0 = [(SEM_ST, None), (SEM_VE, None)]
    actbuf_free = [None, None]
    stage_free = [None, None]
    psum_free = {0: None, 4: None}
    for ep in range(5):
        r.g.wait(*wd_busy_until)
        for e2 in range(2):
            for ib in range(IB):
                r.g.dma(wdt[e2][ib], c.wd[2 * ep + e2, ib], SEM_LW)
        wd_lw = r.n[SEM_LW]
        for q in range(4):
            for e2 in range(2):
                if ep == 0 and q == 0:
                    r.g.wait(SEM_ST, c.act_st[NPASS - 1])
                    r.g.wait(SEM_VE, c.f1_done_ve)
                if actbuf_free[e2] is not None:
                    r.g.wait(*actbuf_free[e2])
                r.g.wait(SEM_ST, c.act_st[2 * ep + e2])
                r.g.dma(_f2_act_view(c, e2),
                        c.act[2 * ep + e2][:, :, q * 1024:(q + 1) * 1024]
                        .rearrange("ib p n -> p ib n"),
                        SEM_LA)
            act_lw = r.n[SEM_LA]
            for tt8 in range(8):
                t = q * 8 + tt8
                grp = (tt8 % 2) * 4
                r.t.wait(SEM_LW, wd_lw)
                r.t.wait(SEM_LA, act_lw)
                if psum_free[grp] is not None:
                    for w_ in psum_free[grp]:
                        r.t.wait(*w_)
                for e2 in range(2):
                    for ib in range(IB):
                        lhsT = actt[e2][ib][:, tt8 * 128:(tt8 + 1) * 128]
                        for of in range(4):
                            ins = r.t.matmul(
                                c.pb[grp + of][:, :], lhsT,
                                wdt[e2][ib][:, of * 512:(of + 1) * 512],
                                start=(e2 == 0 and ib == 0),
                                stop=(e2 == 1 and ib == IB - 1))
                ins.inc(SEM_PE)
                mm_done = r.n[SEM_PE]
                sb_ = (tt8 % 2)
                if stage_free[sb_] is not None:
                    r.v.wait(*stage_free[sb_])
                    r.s.wait(*stage_free[sb_])
                r.v.wait(SEM_PE, mm_done)
                r.s.wait(SEM_PE, mm_done)
                r.v.tensor_copy(stg[sb_][:, 0:512], c.pb[grp + 0][:, :])
                r.v.tensor_copy(stg[sb_][:, 512:1024], c.pb[grp + 1][:, :]).inc(SEM_VE)
                r.s.copy(stg[sb_][:, 1024:1536], c.pb[grp + 2][:, :])
                r.s.copy(stg[sb_][:, 1536:2048], c.pb[grp + 3][:, :]).inc(SEM_SC)
                psum_free[grp] = [(SEM_VE, r.n[SEM_VE]), (SEM_SC, r.n[SEM_SC])]
                r.g.wait(SEM_VE, r.n[SEM_VE])
                r.g.wait(SEM_SC, r.n[SEM_SC])
                if ("t", t) in part_seq:
                    r.g.wait(*part_seq[("t", t)])
                r.g.dma(partv[t], stg[sb_][:, :], SEM_AD,
                        accum_op=(ALU.add if ep > 0 else ALU.bypass))
                part_seq[("t", t)] = (SEM_AD, r.n[SEM_AD])
                stage_free[sb_] = (SEM_AD, r.n[SEM_AD])
            pe_after_q = r.n[SEM_PE]
            actbuf_free = [(SEM_PE, pe_after_q), (SEM_PE, pe_after_q)]
        wd_busy_until = (SEM_PE, r.n[SEM_PE])
    c.part_seq = part_seq
    c.f2_done = (SEM_AD, r.n[SEM_AD])


def _f2_act_view(c, e2):
    v = _carve(c.W, e2 * IB * 2048, BF16, [128, IB * 1024])
    return v.rearrange("p (ib n) -> p ib n", ib=IB)


def _p_ar(c):
    r = c.r
    rows = S // 4
    c.ar_cc = []
    r.g.wait(SEM_LB, r.n[SEM_LB])
    for q in range(4):
        last = c.part_seq[("t", q * 8 + 7)]
        r.g.wait(*last)
        r.g.collective_compute(
            "AllReduce", ALU.add, replica_groups=RG_PAIRS,
            ins=[c.part[q * rows:(q + 1) * rows, :].opt()],
            outs=[c.red[q * rows:(q + 1) * rows, :].opt()]).inc(SEM_CC)
        c.ar_cc.append(r.n[SEM_CC])


def _p_attn_qkv(c):
    r = c.r
    wh = _carve(c.W, 0, BF16, [128, DC * 1024])
    whv = wh.rearrange("p (dc j) -> p dc j", dc=DC)
    rtmp = _carve(c.W, 32768, F32, [128, 1024])
    rop = [_carve(c.W, 36864 + i * 2048, BF16, [128, 1024]) for i in range(2)]
    trs = [_carve(c.W, 40960 + i * 1024, BF16, [128, 512]) for i in range(2)]
    vt_ = [_carve(c.W, 45056 + i * 2048, BF16, [128, 1024]) for i in range(2)]
    cs_ = [_carve(c.W, 49152 + i * 1024, F32, [128, 64]) for i in range(4)]
    t14 = [_carve(c.W, 53248 + i * 2048, F32, [128, 512]) for i in range(4)]
    lnxv = c.LNX[:, :].rearrange("p (dc s) -> p dc s", dc=DC)

    guards = list(c.lnres_done)
    for g_ in guards:
        r.y.wait(*g_)
    wsrc = {"q": c.wq, "k": c.wk, "v": c.wv}
    dfm = {"q": c.qfm, "k": c.kfm}
    w_free = None
    rtmp_free = None
    rop_free = [None, None]
    trs_free = [None, None]
    vt_free = [None, None]
    cs_free = [None, None]
    t14_free = [None, None, None, None]
    mm_banks_free = {0: None, 2: None}
    tp_bank_free = {4: None, 5: None}
    for pj in ("q", "k", "v"):
        for hf in range(2):
            for g_ in guards:
                r.g.wait(*g_)
            if w_free is not None:
                r.g.wait(*w_free)
            r.g.dma(whv, wsrc[pj][:, :, hf * 1024:(hf + 1) * 1024]
                    .rearrange("a b c -> b a c"), SEM_LW)
            guards = []
            lw = r.n[SEM_LW]
            for t in range(TT):
                cb = t % 2
                if pj != "v":
                    if cs_free[cb] is not None:
                        r.y.wait(*cs_free[cb])
                    r.y.dma(cs_[cb], c.ropec[t * 128:(t + 1) * 128, :], SEM_LA)
                    r.y.dma(cs_[2 + cb], c.ropes[t * 128:(t + 1) * 128, :], SEM_LA)
                la = r.n[SEM_LA]
                bk = (t % 2) * 2
                r.t.wait(SEM_LW, lw)
                if mm_banks_free[bk] is not None:
                    r.t.wait(*mm_banks_free[bk])
                for dc in range(DC):
                    for n2 in range(2):
                        ins = r.t.matmul(c.pb[bk + n2][:, :],
                                         lnxv[:, dc, t * 128:(t + 1) * 128],
                                         whv[:, dc, n2 * 512:(n2 + 1) * 512],
                                         start=(dc == 0), stop=(dc == DC - 1))
                ins.inc(SEM_PE)
                mmdone = r.n[SEM_PE]
                if pj == "v":
                    vb = t % 2
                    if vt_free[vb] is not None:
                        r.v.wait(*vt_free[vb])
                    r.v.wait(SEM_PE, mmdone)
                    r.v.tensor_copy(vt_[vb][:, 0:512], c.pb[bk][:, :])
                    r.v.tensor_copy(vt_[vb][:, 512:1024], c.pb[bk + 1][:, :]).inc(SEM_VE)
                    mm_banks_free[bk] = (SEM_VE, r.n[SEM_VE])
                    r.y.wait(SEM_VE, r.n[SEM_VE])
                    r.y.dma(c.vtm[t * 128:(t + 1) * 128,
                                  hf * 1024:(hf + 1) * 1024], vt_[vb][:, :], SEM_ST)
                    vt_free[vb] = (SEM_ST, r.n[SEM_ST])
                    continue
                rb = t % 2
                if rtmp_free is not None:
                    r.v.wait(*rtmp_free)
                r.v.wait(SEM_PE, mmdone)
                r.v.tensor_copy(rtmp[:, 0:512], c.pb[bk][:, :])
                r.v.tensor_copy(rtmp[:, 512:1024], c.pb[bk + 1][:, :]).inc(SEM_VE)
                mm_banks_free[bk] = (SEM_VE, r.n[SEM_VE])
                ev = rtmp.rearrange("p (h z) -> p h z", h=8)[:, :, 0:64]
                ov = rtmp.rearrange("p (h z) -> p h z", h=8)[:, :, 64:128]
                cosb = cs_[cb][:, 0:64].unsqueeze(1).broadcast_to([128, 8, 64])
                sinb = cs_[2 + cb][:, 0:64].unsqueeze(1).broadcast_to([128, 8, 64])
                r.g.wait(SEM_VE, r.n[SEM_VE])
                r.g.wait(SEM_LA, la)
                for i4 in range(4):
                    if t14_free[i4] is not None:
                        r.g.wait(*t14_free[i4])
                t1v = [t14[i].rearrange("p (h z) -> p h z", h=8) for i in range(4)]
                r.g.tensor_mul(t1v[0], ev, cosb)
                r.g.tensor_mul(t1v[1], ov, sinb)
                r.g.tensor_mul(t1v[2], ev, sinb)
                r.g.tensor_mul(t1v[3], ov, cosb).inc(SEM_GP)
                r.g.wait(SEM_GP, r.n[SEM_GP])
                if rop_free[rb] is not None:
                    r.g.wait(*rop_free[rb])
                rv = rop[rb].rearrange("p (h z) -> p h z", h=8)
                r.g.tensor_sub(rv[:, :, 0:64], t1v[0], t1v[1])
                r.g.tensor_add(rv[:, :, 64:128], t1v[2], t1v[3]).inc(SEM_GP)
                rtmp_free = (SEM_GP, r.n[SEM_GP])
                for i4 in range(4):
                    t14_free[i4] = (SEM_GP, r.n[SEM_GP])
                for g4 in range(2):
                    tb = 4 + g4
                    pv = c.pb[tb][:, :].bitcast(BF16)[:, :512]
                    if tp_bank_free[tb] is not None:
                        r.t.wait(*tp_bank_free[tb])
                    r.t.wait(SEM_GP, r.n[SEM_GP])
                    for k4 in range(4):
                        hh = g4 * 4 + k4
                        ins = r.t.transpose(
                            pv.rearrange("p (a b) -> p a b", a=4)[:, k4, :],
                            rop[rb][:, hh * 128:(hh + 1) * 128],
                            c.identb_sb)
                    ins.inc(SEM_PE)
                    tr = trs[g4]
                    if trs_free[g4] is not None:
                        r.v.wait(*trs_free[g4])
                    r.v.wait(SEM_PE, r.n[SEM_PE])
                    r.v.tensor_copy(tr.rearrange("p (a b) -> p a b", a=4),
                                    pv.rearrange("p (a b) -> p a b", a=4)).inc(SEM_VE)
                    tp_bank_free[tb] = (SEM_VE, r.n[SEM_VE])
                    r.y.wait(SEM_VE, r.n[SEM_VE])
                    r.y.dma(dfm[pj][hf * 8 + g4 * 4: hf * 8 + g4 * 4 + 4, :,
                                    t * 128:(t + 1) * 128]
                            .rearrange("h p n -> p h n"),
                            tr.rearrange("p (h n) -> p h n", h=4), SEM_ST)
                    trs_free[g4] = (SEM_ST, r.n[SEM_ST])
                rop_free[rb] = (SEM_PE, r.n[SEM_PE])
            w_free = (SEM_PE, r.n[SEM_PE])
    c.qkv_done_st = r.n[SEM_ST]
    c.qkv_done_pe = r.n[SEM_PE]


def _p_attn(c):
    r = c.r
    qb = [_carve(c.LNX, i * 8192, BF16, [128, S]) for i in range(2)]
    kb = [_carve(c.LNX, 16384 + i * 8192, BF16, [128, S]) for i in range(2)]
    vb = [_carve(c.LNX, 32768 + i * 8192, BF16, [128, S]) for i in range(2)]
    pt = [_carve(c.LNX, 49152 + i * 2048, BF16, [128, 1024]) for i in range(2)]
    rcpb = _carve(c.LNX, 53248, F32, [128, 512])
    cxo = [_carve(c.LNX, 55296 + i * 1024, BF16, [128, 512]) for i in range(2)]
    msk = _carve(c.LNX, 57344, F32, [128, 4 * 512])
    onescol = _carve(c.LNX, 65536, BF16, [128, 1])
    rcp = _carve(c.LNX, 66560, F32, [1, 512])
    spair = [c.pb01[:, :], c.pb23[:, :]]

    r.g.wait(SEM_PE, c.qkv_done_pe)
    r.g.wait(SEM_ST, c.qkv_done_st)
    r.g.dma(msk.rearrange("p (a b) -> p a b", a=4),
            c.maskT.rearrange("a b c -> b a c"), SEM_LW)
    r.g.memset(onescol[:, :], 1.0).inc(SEM_GP)
    mask_lw = r.n[SEM_LW]
    ones_gp = r.n[SEM_GP]

    r.y.wait(SEM_PE, c.qkv_done_pe)
    r.y.wait(SEM_ST, c.qkv_done_st)
    hbuf_free = [None, None]
    lw_of_h = {}

    def load_h(h):
        hb = h % 2
        if hbuf_free[hb] is not None:
            r.y.wait(*hbuf_free[hb])
        r.y.dma(qb[hb][:, :], c.qfm[h], SEM_LA)
        r.y.dma(kb[hb][:, :], c.kfm[h], SEM_LA)
        r.y.dma(vb[hb][:, :].rearrange("p (kt n) -> p kt n", kt=TT),
                c.vtm.rearrange("(kt p) d -> p kt d", p=128)
                [:, :, h * 128:(h + 1) * 128], SEM_LA)
        return r.n[SEM_LA]

    lw_of_h[0] = load_h(0)
    pair_free = [None, None]
    pt_free = [None, None]
    ctx_bank_free = {4: None, 5: None}
    sum_bank_free = {6: None, 7: None}
    rcp_free = None
    rcpb_free = None
    cxo_free = [None, None]
    for h in range(AH):
        hb = h % 2
        la = lw_of_h.pop(h)
        if h + 1 < AH:
            lw_of_h[h + 1] = load_h(h + 1)
        for qg in range(8):
            cbk = 4 + (qg % 2)
            sbk = 6 + (qg % 2)
            nkt = (qg + 1) * 4
            npair = (nkt + 1) // 2
            exp_of_pair = {}
            for pi in range(npair + 1):
                if pi < npair:
                    pr = pi % 2
                    sc = spair[pr]
                    kts = [kt for kt in (2 * pi, 2 * pi + 1) if kt < nkt]
                    r.t.wait(SEM_LA, la)
                    if pair_free[pr] is not None:
                        r.t.wait(*pair_free[pr])
                    gate = None
                    for j, kt in enumerate(kts):
                        r.t.matmul(sc[:, j * 512:(j + 1) * 512],
                                   kb[hb][:, kt * 128:(kt + 1) * 128],
                                   qb[hb][:, qg * 512:(qg + 1) * 512],
                                   start=True, stop=True).inc(SEM_PE)
                        sc_mm = r.n[SEM_PE]
                        if kt >= qg * 4:
                            jj = kt - qg * 4
                            r.v.wait(SEM_PE, sc_mm)
                            if h == 0 and qg == 0 and kt == 0:
                                r.v.wait(SEM_LW, mask_lw)
                            r.v.tensor_add(sc[:, j * 512:(j + 1) * 512],
                                           sc[:, j * 512:(j + 1) * 512],
                                           msk[:, jj * 512:(jj + 1) * 512]
                                           ).inc(SEM_VE)
                            gate = (SEM_VE, r.n[SEM_VE])
                    if gate is None:
                        gate = (SEM_PE, r.n[SEM_PE])
                    if pt_free[pr] is not None:
                        r.s.wait(*pt_free[pr])
                    r.s.wait(*gate)
                    w = len(kts) * 512
                    r.s.activation(pt[pr][:, 0:w], sc[:, 0:w],
                                   AF.Exp, bias=c.consts[:, 6:7],
                                   scale=ASCALE).inc(SEM_SC)
                    exp_of_pair[pi] = r.n[SEM_SC]
                cp = pi - 1
                if cp >= 0:
                    pr2 = cp % 2
                    kts2 = [kt for kt in (2 * cp, 2 * cp + 1) if kt < nkt]
                    r.t.wait(SEM_SC, exp_of_pair[cp])
                    if cp == 0:
                        if ctx_bank_free[cbk] is not None:
                            r.t.wait(*ctx_bank_free[cbk])
                        if sum_bank_free[sbk] is not None:
                            r.t.wait(*sum_bank_free[sbk])
                        if h == 0 and qg == 0:
                            r.t.wait(SEM_GP, ones_gp)
                    for j, kt in enumerate(kts2):
                        r.t.matmul(c.pb[cbk][:, :],
                                   vb[hb][:, kt * 128:(kt + 1) * 128],
                                   pt[pr2][:, j * 512:(j + 1) * 512],
                                   start=(kt == 0), stop=(kt == nkt - 1))
                        r.t.matmul(c.pb[sbk][:1, :], onescol[:, :],
                                   pt[pr2][:, j * 512:(j + 1) * 512],
                                   start=(kt == 0),
                                   stop=(kt == nkt - 1)).inc(SEM_PE)
                    pt_free[pr2] = (SEM_PE, r.n[SEM_PE])
                    pair_free[pr2] = (SEM_SC, exp_of_pair.get(pi, r.n[SEM_SC])) \
                        if False else pair_free[pr2]
            ctx_done = r.n[SEM_PE]
            if rcp_free is not None:
                r.v.wait(*rcp_free)
            r.v.wait(SEM_PE, ctx_done)
            r.v.reciprocal(rcp[:, :], c.pb[sbk][:1, :]).inc(SEM_VE)
            r.t.wait(SEM_VE, r.n[SEM_VE])
            r.t.matmul(c.pb[sbk][:, :], c.onesf_sb[:, :], rcp[:, :],
                       start=True, stop=True).inc(SEM_PE)
            rcp_free = (SEM_PE, r.n[SEM_PE])
            if rcpb_free is not None:
                r.v.wait(*rcpb_free)
            r.v.wait(SEM_PE, r.n[SEM_PE])
            r.v.tensor_copy(rcpb[:, :], c.pb[sbk][:, :]).inc(SEM_VE)
            sum_bank_free[sbk] = (SEM_VE, r.n[SEM_VE])
            co = cxo[qg % 2]
            if cxo_free[qg % 2] is not None:
                r.v.wait(*cxo_free[qg % 2])
            r.v.wait(SEM_VE, r.n[SEM_VE])
            r.v.tensor_mul(co[:, :], c.pb[cbk][:, :], rcpb[:, :]).inc(SEM_VE)
            rcpb_free = (SEM_VE, r.n[SEM_VE])
            ctx_bank_free[cbk] = (SEM_VE, r.n[SEM_VE])
            r.y.wait(SEM_VE, r.n[SEM_VE])
            r.y.dma(c.cfm[h][:, qg * 512:(qg + 1) * 512], co[:, :], SEM_ST)
            cxo_free[qg % 2] = (SEM_ST, r.n[SEM_ST])
        hbuf_free[hb] = (SEM_PE, r.n[SEM_PE])
    c.attn_done_pe = r.n[SEM_PE]
    c.attn_done_st = r.n[SEM_ST]
    c.attn_done_ve = r.n[SEM_VE]


def _p_wores(c, srcfm, wsrc, xsrc, xdst, nhb=16):
    r = c.r
    wot = [_carve(c.LNX, hb * 4096, BF16, [128, D]) for hb in range(nhb)]
    ctile = [_carve(c.W, i * 4096, BF16, [128, nhb * 128]) for i in range(2)]
    xtl = [_carve(c.W, 16384 + i * 8192, F32, [128, D]) for i in range(2)]
    stg = [_carve(c.W, 32768 + i * 8192, F32, [128, D]) for i in range(2)]

    for g_ in c.wores_guards:
        r.g.wait(*g_)
    for hb in range(nhb):
        r.g.dma(wot[hb], wsrc[hb], SEM_LW)
    lw = r.n[SEM_LW]
    xv = xsrc.rearrange("(t p) d -> t p d", p=128)
    dv = xdst.rearrange("(t p) d -> t p d", p=128)
    ct_free = [None, None]
    xt_free = [None, None]
    st_free = [None, None]
    bank_free = {0: None, 4: None}
    for t in range(TT):
        bi = t % 2
        if ct_free[bi] is not None:
            r.y.wait(*ct_free[bi])
        r.y.dma(ctile[bi].rearrange("p (h n) -> p h n", h=nhb),
                srcfm[:, :, t * 128:(t + 1) * 128].rearrange("h p n -> p h n"),
                SEM_LA)
        if xt_free[bi] is not None:
            r.y.wait(*xt_free[bi])
        r.y.dma(xtl[bi], xv[t], SEM_LB)
        la, lb = r.n[SEM_LA], r.n[SEM_LB]
        grp = (t % 2) * 4
        r.t.wait(SEM_LA, la)
        if t == 0:
            r.t.wait(SEM_LW, lw)
        if bank_free[grp] is not None:
            for w_ in bank_free[grp]:
                r.t.wait(*w_)
        cv = ctile[bi].rearrange("p (h n) -> p h n", h=nhb)
        for hb in range(nhb):
            for of in range(4):
                ins = r.t.matmul(c.pb[grp + of][:, :], cv[:, hb, :],
                                 wot[hb][:, of * 512:(of + 1) * 512],
                                 start=(hb == 0), stop=(hb == nhb - 1))
        ins.inc(SEM_PE)
        mm = r.n[SEM_PE]
        if st_free[bi] is not None:
            r.v.wait(*st_free[bi])
        r.v.wait(SEM_PE, mm)
        r.v.wait(SEM_LB, lb)
        for of in range(4):
            ins = r.v.tensor_add(stg[bi][:, of * 512:(of + 1) * 512],
                                 xtl[bi][:, of * 512:(of + 1) * 512],
                                 c.pb[grp + of][:, :])
        ins.inc(SEM_VE)
        bank_free[grp] = [(SEM_VE, r.n[SEM_VE])]
        xt_free[bi] = (SEM_VE, r.n[SEM_VE])
        ct_free[bi] = (SEM_PE, mm)
        r.y.wait(SEM_VE, r.n[SEM_VE])
        r.y.dma(dv[t], stg[bi][:, :], SEM_ST)
        st_free[bi] = (SEM_ST, r.n[SEM_ST])
    c.wores_done = [(SEM_PE, r.n[SEM_PE]), (SEM_VE, r.n[SEM_VE]),
                    (SEM_ST, r.n[SEM_ST])]


def _p_ssd_proj(c):
    r = c.r
    wpj = [_carve(c.W, i * 16384, BF16, [128, DC * 512]) for i in range(2)]
    stg = [_carve(c.W, 32768 + i * 1024, BF16, [128, 512]) for i in range(2)]
    trs = [_carve(c.W, 34816 + i * 1024, BF16, [128, 512]) for i in range(2)]
    guards = list(c.lnres_done)
    for g_ in guards:
        r.g.wait(*g_)
        r.y.wait(*g_)
    lnxv = c.LNX[:, :].rearrange("p (dc s) -> p dc s", dc=DC)
    wbuf_free = [None, None]
    stg_free = [None, None]
    trs_free = [None, None]
    mmb_free = {0: None, 1: None}
    tpb_free = {4: None, 5: None}
    dt_written = 0
    for cg in range(11):
        wb = cg % 2
        ncols = 512 if cg < 10 else 32
        if wbuf_free[wb] is not None:
            r.g.wait(*wbuf_free[wb])
        r.g.dma(wpj[wb].rearrange("p (dc j) -> p dc j", dc=DC)[:, :, 0:ncols],
                c.wproj[:, :, cg * 512:cg * 512 + ncols]
                .rearrange("a b c -> b a c"), SEM_LW)
        lw = r.n[SEM_LW]
        wv = wpj[wb].rearrange("p (dc j) -> p dc j", dc=DC)
        for t in range(TT):
            bk = t % 2
            r.t.wait(SEM_LW, lw)
            if mmb_free[bk] is not None:
                r.t.wait(*mmb_free[bk])
            for dc in range(DC):
                ins = r.t.matmul(c.pb[bk][:, 0:ncols],
                                 lnxv[:, dc, t * 128:(t + 1) * 128],
                                 wv[:, dc, 0:ncols],
                                 start=(dc == 0), stop=(dc == DC - 1))
            ins.inc(SEM_PE)
            mm = r.n[SEM_PE]
            if cg == 10:
                r.v.wait(SEM_PE, mm)
                r.v.tensor_copy(c.dttm[:, t * 32:(t + 1) * 32],
                                c.pb[bk][:, 0:32]).inc(SEM_VE)
                mmb_free[bk] = (SEM_VE, r.n[SEM_VE])
                dt_written = r.n[SEM_VE]
                continue
            sb_ = t % 2
            if stg_free[sb_] is not None:
                r.v.wait(*stg_free[sb_])
            r.v.wait(SEM_PE, mm)
            r.v.tensor_copy(stg[sb_][:, :], c.pb[bk][:, 0:512]).inc(SEM_VE)
            mmb_free[bk] = (SEM_VE, r.n[SEM_VE])
            tb = 4 + (t % 2)
            pv = c.pb[tb][:, :].bitcast(BF16)[:, :512]
            if tpb_free[tb] is not None:
                r.t.wait(*tpb_free[tb])
            r.t.wait(SEM_VE, r.n[SEM_VE])
            for k4 in range(4):
                ins = r.t.transpose(
                    pv.rearrange("p (a b) -> p a b", a=4)[:, k4, :],
                    stg[sb_][:, k4 * 128:(k4 + 1) * 128], c.identb_sb)
            ins.inc(SEM_PE)
            stg_free[sb_] = (SEM_PE, r.n[SEM_PE])
            if trs_free[sb_] is not None:
                r.v.wait(*trs_free[sb_])
            r.v.wait(SEM_PE, r.n[SEM_PE])
            r.v.tensor_copy(trs[sb_].rearrange("p (a b) -> p a b", a=4),
                            pv.rearrange("p (a b) -> p a b", a=4)).inc(SEM_VE)
            tpb_free[tb] = (SEM_VE, r.n[SEM_VE])
            dst = c.vkqpre if cg < 6 else c.gfm
            blk0 = cg * 4 if cg < 6 else (cg - 6) * 4
            r.y.wait(SEM_VE, r.n[SEM_VE])
            r.y.dma(dst[blk0:blk0 + 4, :, t * 128:(t + 1) * 128]
                    .rearrange("h p n -> p h n"),
                    trs[sb_].rearrange("p (h n) -> p h n", h=4), SEM_ST)
            trs_free[sb_] = (SEM_ST, r.n[SEM_ST])
        wbuf_free[wb] = (SEM_PE, r.n[SEM_PE])
    c.proj_done = [(SEM_PE, r.n[SEM_PE]), (SEM_ST, r.n[SEM_ST]),
                   (SEM_VE, dt_written)]


def _p_ssd_conv(c):
    r = c.r
    cv = [_carve(c.W, i * 8192, BF16, [128, S]) for i in range(2)]
    t1 = _carve(c.W, 16384, F32, [128, S])
    t2 = _carve(c.W, 45056, F32, [128, S])
    ob = [_carve(c.W, 32768, BF16, [128, S]),
          _carve(c.W, 61440, BF16, [128, S])]

    for g_ in c.proj_done:
        r.y.wait(*g_)
    cv_free = [None, None]
    t_free = None
    ob_free = [None, None]
    for cb in range(24):
        bi = cb % 2
        if cv_free[bi] is not None:
            r.y.wait(*cv_free[bi])
        r.y.dma(cv[bi][:, :], c.vkqpre[cb], SEM_LA)
        la = r.n[SEM_LA]
        w0 = c.convp_sb[:, 2 * cb:2 * cb + 1]
        w1 = c.convp_sb[:, 2 * cb + 1:2 * cb + 2]
        b_ = c.convb_sb[:, cb:cb + 1]
        r.v.wait(SEM_LA, la)
        if t_free is not None:
            r.v.wait(*t_free)
        r.v.tensor_scalar(t1[:, :], cv[bi][:, :], w1, b_,
                          ALU.mult, ALU.add).inc(SEM_VE)
        r.v.wait(SEM_VE, r.n[SEM_VE])
        r.v.scalar_tensor_tensor(t2[:, 1:S], cv[bi][:, 0:S - 1], w0,
                                 t1[:, 1:S], ALU.mult, ALU.add)
        r.v.tensor_copy(t2[:, 0:1], t1[:, 0:1]).inc(SEM_VE)
        cv_free[bi] = (SEM_VE, r.n[SEM_VE])
        if ob_free[bi] is not None:
            r.s.wait(*ob_free[bi])
        r.s.wait(SEM_VE, r.n[SEM_VE])
        r.s.activation(ob[bi][:, :], t2[:, :], AF.Silu).inc(SEM_SC)
        t_free = (SEM_SC, r.n[SEM_SC])
        r.y.wait(SEM_SC, r.n[SEM_SC])
        r.y.dma(c.vkqpost[cb], ob[bi][:, :], SEM_ST)
        ob_free[bi] = (SEM_ST, r.n[SEM_ST])
    c.conv_done = [(SEM_SC, r.n[SEM_SC]), (SEM_ST, r.n[SEM_ST]),
                   (SEM_VE, r.n[SEM_VE])]


PREP_CUT = None
SSD_DEV = None


def _p_ssd_prep(c):
    r = c.r
    L = c.LNX
    dtfm = _carve(L, 0, F32, [SH, S])
    sp1 = _carve(L, 16384, F32, [SH, S])
    sp2 = _carve(L, 96 * 1024, F32, [SH, S])
    acs = _carve(L, 32768, F32, [SH, S])
    eacs = _carve(L, 49152, F32, [SH, S])
    dstf = _carve(L, 65536, F32, [SH, S])
    smask = _carve(L, 80 * 1024, F32, [SH, S])
    acst = _carve(c.W, 45056, F32, [128, 1024])
    eacst = _carve(c.W, 49152, F32, [128, 1024])
    dstt = _carve(c.W, 53248, F32, [128, 1024])
    dtvt = _carve(c.W, 57344, F32, [128, 1024])
    dab = _carve(c.W, 61440, F32, [128, 1024])
    expal = _carve(c.sm, 2304, F32, [SH, 1])
    for g_ in c.proj_done + c.conv_done:
        r.g.wait(*g_)
    r.g.dma(smask, c.scanmask, SEM_LA)
    for t in range(TT):
        bk = t % 2
        if t >= 2:
            r.t.wait(SEM_VE, ve_evac[t - 2])
        if t == 0:
            for g_ in c.proj_done + c.conv_done:
                r.t.wait(*g_)
            ve_evac = {}
        r.t.transpose(c.pb[bk][:32, :128], c.dttm[:, t * 32:(t + 1) * 32],
                      c.identf_sb).inc(SEM_PE)
        r.v.wait(SEM_PE, r.n[SEM_PE])
        r.v.tensor_copy(dtfm[:, t * 128:(t + 1) * 128],
                        c.pb[bk][:32, :128]).inc(SEM_VE)
        ve_evac[t] = r.n[SEM_VE]
    if PREP_CUT == "T":
        return
    r.v.wait(SEM_VE, r.n[SEM_VE])
    r.v.tensor_scalar(sp1, dtfm, c.dtb_sb[:, 0:1], None, ALU.add).inc(SEM_VE)
    r.s.wait(SEM_VE, r.n[SEM_VE])
    r.s.activation(sp2, sp1, AF.Abs).inc(SEM_SC)
    r.s.wait(SEM_SC, r.n[SEM_SC])
    r.s.activation(sp2, sp2, AF.Exp, scale=-1.0).inc(SEM_SC)
    r.s.wait(SEM_SC, r.n[SEM_SC])
    r.s.activation(sp2, sp2, AF.Ln, bias=c.consts[:SH, 3:4]).inc(SEM_SC)
    r.v.wait(SEM_SC, r.n[SEM_SC])
    r.v.tensor_scalar_max(dtfm, sp1, c.consts[:SH, 4:5]).inc(SEM_VE)
    r.v.wait(SEM_VE, r.n[SEM_VE])
    r.v.tensor_add(dtfm, dtfm, sp2).inc(SEM_VE)
    r.s.activation(expal, c.alb_sb[:, 0:1], AF.Exp).inc(SEM_SC)
    r.v.wait(SEM_SC, r.n[SEM_SC])
    r.v.wait(SEM_VE, r.n[SEM_VE])
    r.v.tensor_scalar_mul(expal, expal, c.consts[:SH, 5:6]).inc(SEM_VE)
    r.v.wait(SEM_VE, r.n[SEM_VE])
    r.v.tensor_scalar_mul(sp1, dtfm, expal[:, 0:1]).inc(SEM_VE)
    if PREP_CUT == "SP":
        return
    r.v.wait(SEM_LA, r.n[SEM_LA])
    r.v.wait(SEM_VE, r.n[SEM_VE])
    r.v.tensor_tensor_scan(acs, smask, sp1, 0.0, ALU.mult, ALU.add).inc(SEM_VE)
    r.s.wait(SEM_VE, r.n[SEM_VE])
    r.s.activation(eacs, acs, AF.Exp).inc(SEM_SC)
    lastv = acs.rearrange("h (t n) -> h t n", n=128)[:, :, 127:128]
    r.v.wait(SEM_SC, r.n[SEM_SC])
    r.v.tensor_sub(sp2.rearrange("h (t n) -> h t n", n=128),
                   lastv.broadcast_to([SH, TT, 128]),
                   acs.rearrange("h (t n) -> h t n", n=128)).inc(SEM_VE)
    r.s.wait(SEM_VE, r.n[SEM_VE])
    r.s.activation(dstf, sp2, AF.Exp).inc(SEM_SC)
    prep_sc = r.n[SEM_SC]
    if PREP_CUT == "SCAN":
        return
    for t in range(TT):
        for i, (src_, dst_) in enumerate(
                [(acs, acst), (eacs, eacst), (dstf, dstt), (dtfm, dtvt)]):
            bk = 2 + (i % 2)
            r.t.wait(SEM_SC, prep_sc)
            r.t.wait(SEM_VE, r.n[SEM_VE])
            r.t.transpose(c.pb[bk][:, 0:32],
                          src_[:, t * 128:(t + 1) * 128],
                          c.identf_sb[:SH, 0:SH]).inc(SEM_PE)
            r.v.wait(SEM_PE, r.n[SEM_PE])
            r.v.tensor_copy(dst_[:, t * 32:(t + 1) * 32],
                            c.pb[bk][:, 0:32]).inc(SEM_VE)
    if PREP_CUT == "TM":
        return
    r.y.wait(SEM_VE, r.n[SEM_VE])
    r.y.dma(c.dAd, eacst[127:128, :], SEM_ST)
    if PREP_CUT == "DA1":
        return
    r.y.wait(SEM_ST, r.n[SEM_ST])
    r.y.dma(c.rowb[0:1, 0:1024], c.dAd, SEM_LA)
    r.y.dma(c.acsd, acs, SEM_ST)
    la = r.n[SEM_LA]
    if PREP_CUT == "DA2":
        return
    r.t.wait(SEM_LA, la)
    for q in range(2):
        if q > 0:
            r.t.wait(SEM_VE, r.n[SEM_VE])
        r.t.matmul(c.pb[6][:, :], c.onesf_sb[:, :],
                   c.rowb[0:1, q * 512:(q + 1) * 512],
                   start=True, stop=True).inc(SEM_PE)
        r.v.wait(SEM_PE, r.n[SEM_PE])
        r.v.tensor_copy(dab[:, q * 512:(q + 1) * 512],
                        c.pb[6][:, :]).inc(SEM_VE)
    c.prep_done = [(SEM_VE, r.n[SEM_VE]), (SEM_SC, r.n[SEM_SC]),
                   (SEM_ST, r.n[SEM_ST]), (SEM_PE, r.n[SEM_PE])]
    c.ssd_tm = (acst, eacst, dstt, dtvt, dab)


def _p_ssd_heads(c):
    r = c.r
    L = c.LNX
    acst, eacst, dstt, dtvt, dab = c.ssd_tm
    Bf = _carve(L, 0, BF16, [64, S])
    Vf = _carve(L, 8192, BF16, [64, S])
    Cf = _carve(L, 16384, BF16, [64, S])
    gf = _carve(L, 24576, BF16, [64, S])
    yst = _carve(L, 16384, BF16, [64, S])
    Btm = _carve(L, 96 * 1024, BF16, [128, NCH * 64])
    Vtm = _carve(L, 100 * 1024, BF16, [128, NCH * 64])
    xtm = _carve(L, 104 * 1024, BF16, [128, NCH * 64])
    Bwt = _carve(L, 108 * 1024, BF16, [128, NCH * 64])
    acsb = _carve(L, 112 * 1024, F32, [128, S])
    sqb = _carve(L, 80 * 1024, BF16, [128, NCH * 64])
    ygb = _carve(L, 84 * 1024, BF16, [128, NCH * 64])
    msb = _carve(L, 88 * 1024, F32, [128, 64])
    LT = _carve(c.W, 0, F32, [128, S])
    gtm = _carve(c.W, 67584, BF16, [128, NCH * 64])
    Mt = [_carve(c.W, 65536 + i * 1024, BF16, [128, 512]) for i in range(2)]
    ydt = _carve(c.sm, 5632, F32, [128, 256])
    carry = [_carve(c.sm, 2432 + i * 256, F32, [64, 64]) for i in range(2)]
    carryb = [_carve(c.sm, 2944 + i * 128, BF16, [64, 64]) for i in range(2)]
    ysb = _carve(c.W, 32768, F32, [128, NCH * 64])

    for g_ in c.prep_done:
        r.y.wait(*g_)
        r.g.wait(*g_)
    kv_free = None
    cf_free = None
    gf_free = None
    row_free = None
    tm_free = None
    acsb_free = None
    LT_free = None
    Mt_free = [None, None]
    bank_free = {}
    ysb_free = None
    ydt_free = None
    yst_free = None
    sq_free = None
    yg_free = None
    gtm_free = None
    for kvh in range(SKVH):
        if kv_free is not None:
            r.y.wait(*kv_free)
        vcb, vhf = kvh // 2, kvh % 2
        r.y.dma(Bf[:, :], c.vkqpost[4 + vcb][vhf * 64:vhf * 64 + 64, :], SEM_LA)
        r.y.dma(Vf[:, :], c.vkqpost[vcb][vhf * 64:vhf * 64 + 64, :], SEM_LA)
        kv_la = r.n[SEM_LA]
        if tm_free is not None:
            r.t.wait(*tm_free)
        r.t.wait(SEM_LA, kv_la)
        for cch in range(NCH):
            bk = cch % 2
            pv = c.pb[bk][:, :].bitcast(BF16)
            if bank_free.get(("tm", bk)) is not None:
                r.t.wait(*bank_free[("tm", bk)])
            r.t.transpose(pv[:, 0:64], Bf[:, cch * 128:(cch + 1) * 128],
                          c.identb_sb[:64, 0:64])
            r.t.transpose(pv[:, 64:128], Vf[:, cch * 128:(cch + 1) * 128],
                          c.identb_sb[:64, 0:64]).inc(SEM_PE)
            r.v.wait(SEM_PE, r.n[SEM_PE])
            r.v.tensor_copy(Btm[:, cch * 64:(cch + 1) * 64], pv[:, 0:64])
            r.v.tensor_copy(Vtm[:, cch * 64:(cch + 1) * 64],
                            pv[:, 64:128]).inc(SEM_VE)
            bank_free[("tm", bk)] = (SEM_VE, r.n[SEM_VE])
        btm_ve = r.n[SEM_VE]
        for hh in range(4):
            h = kvh * 4 + hh
            qcb, qhf = (1024 + h * 64) // 128, (h * 64 // 64) % 2
            if cf_free is not None:
                r.y.wait(*cf_free)
            r.y.dma(Cf[:, :], c.vkqpost[qcb][qhf * 64:qhf * 64 + 64, :], SEM_LA)
            if gf_free is not None:
                r.y.wait(*gf_free)
            gcb, ghf = (h * 64) // 128, h % 2
            r.y.dma(gf[:, :], c.gfm[gcb][ghf * 64:ghf * 64 + 64, :], SEM_LA)
            if row_free is not None:
                r.y.wait(*row_free)
            r.y.dma(c.rowb[0:1, :], c.acsd[h:h + 1, :], SEM_LA)
            h_la = r.n[SEM_LA]
            dtv_v = dtvt.rearrange("p (t j) -> p t j", j=32)[:, :, h:h + 1]
            dst_v = dstt.rearrange("p (t j) -> p t j", j=32)[:, :, h:h + 1]
            eac_v = eacst.rearrange("p (t j) -> p t j", j=32)[:, :, h:h + 1]
            acs_v = acst.rearrange("p (t j) -> p t j", j=32)[:, :, h:h + 1]
            x3 = xtm.rearrange("p (t j) -> p t j", j=64)
            bw3 = Bwt.rearrange("p (t j) -> p t j", j=64)
            bt3 = Btm.rearrange("p (t j) -> p t j", j=64)
            vt3 = Vtm.rearrange("p (t j) -> p t j", j=64)
            r.v.wait(SEM_VE, btm_ve)
            prev_mm = bank_free.get(("xw",))
            if prev_mm is not None:
                r.v.wait(*prev_mm)
            r.v.tensor_mul(x3, vt3, dtv_v.broadcast_to([128, NCH, 64]))
            r.v.tensor_mul(bw3, bt3, dst_v.broadcast_to([128, NCH, 64])).inc(SEM_VE)
            xw_ve = r.n[SEM_VE]
            r.t.wait(SEM_LA, h_la)
            r.t.wait(SEM_VE, xw_ve)
            for q8 in range(8):
                bk = 6 + (q8 % 2)
                if bank_free.get(("b67", bk)) is not None:
                    r.t.wait(*bank_free[("b67", bk)])
                r.t.matmul(c.pb[bk][:, :], c.onesf_sb[:, :],
                           c.rowb[0:1, q8 * 512:(q8 + 1) * 512],
                           start=True, stop=True).inc(SEM_PE)
                if acsb_free is not None and q8 == 0:
                    r.v.wait(*acsb_free)
                r.v.wait(SEM_PE, r.n[SEM_PE])
                r.v.tensor_copy(acsb[:, q8 * 512:(q8 + 1) * 512],
                                c.pb[bk][:, :]).inc(SEM_VE)
                bank_free[("b67", bk)] = (SEM_VE, r.n[SEM_VE])
            row_free = (SEM_PE, r.n[SEM_PE])
            if LT_free is not None:
                r.v.wait(*LT_free)
            r.v.wait(SEM_VE, r.n[SEM_VE])
            r.v.tensor_sub(LT.rearrange("p (t l) -> p t l", l=128),
                           acsb.rearrange("p (t l) -> p t l", l=128),
                           acs_v.broadcast_to([128, NCH, 128])).inc(SEM_VE)
            r.v.wait(SEM_VE, r.n[SEM_VE])
            r.v.tensor_add(LT.rearrange("p (t l) -> p t l", l=128),
                           LT.rearrange("p (t l) -> p t l", l=128),
                           c.segneg_sb[:, :].unsqueeze(1)
                           .broadcast_to([128, NCH, 128])).inc(SEM_VE)
            acsb_free = (SEM_VE, r.n[SEM_VE])
            r.s.wait(SEM_VE, r.n[SEM_VE])
            r.s.activation(LT, LT, AF.Exp).inc(SEM_SC)
            lt_sc = r.n[SEM_SC]
            r.v.memset(carry[0][:, :], 0.0)
            r.v.memset(carryb[0][:, :], 0.0).inc(SEM_VE)
            stt_prev = r.n[SEM_VE]
            for cq in range(8):
                bcb = (cq % 2)
                if bank_free.get(("bcq", bcb)) is not None:
                    r.t.wait(*bank_free[("bcq", bcb)])
                for c4 in range(4):
                    cch = cq * 4 + c4
                    ins = r.t.matmul(
                        c.pb[bcb][:, :].rearrange("p (a b) -> p a b", a=4)[:, c4, :],
                        Bf[:, cch * 128:(cch + 1) * 128],
                        Cf[:, cch * 128:(cch + 1) * 128],
                        start=(c4 == 0), stop=(c4 == 3))
                ins.inc(SEM_PE)
                bcq_pe = r.n[SEM_PE]
                mb = Mt[cq % 2]
                if Mt_free[cq % 2] is not None:
                    r.v.wait(*Mt_free[cq % 2])
                r.v.wait(SEM_PE, bcq_pe)
                r.v.wait(SEM_SC, lt_sc)
                r.v.tensor_mul(mb[:, :], c.pb[bcb][:, :],
                               LT[:, cq * 512:(cq + 1) * 512]).inc(SEM_VE)
                mt_ve = r.n[SEM_VE]
                bank_free[("bcq", bcb)] = (SEM_VE, mt_ve)
                ydb = 2 + (cq % 2)
                ssb = 5
                if bank_free.get(("yd", ydb)) is not None:
                    r.t.wait(*bank_free[("yd", ydb)])
                if bank_free.get(("yo",)) is not None:
                    r.t.wait(*bank_free[("yo",)])
                r.t.wait(SEM_VE, mt_ve)
                for c4 in range(4):
                    cch = cq * 4 + c4
                    r.t.matmul(c.pb[ydb][:, :].rearrange(
                        "p (a b) -> p a b", a=8)[:, c4, :],
                        mb[:, c4 * 128:(c4 + 1) * 128],
                        x3[:, cch, :], start=(c4 == 0), stop=(c4 == 3))
                    r.t.wait(SEM_VE, stt_prev)
                    r.t.matmul(c.pb[ssb][:64, 0:64], bw3[:, cch, :],
                               x3[:, cch, :], start=True, stop=True).inc(SEM_PE)
                    s_pe = r.n[SEM_PE]
                    cin = carry[cch % 2]
                    cout = carry[(cch + 1) % 2]
                    yob = 6 + (c4 % 2)
                    if bank_free.get(("b67", yob)) is not None:
                        r.t.wait(*bank_free[("b67", yob)])
                    r.t.matmul(c.pb[yob][:, 0:64],
                        Cf[:, cch * 128:(cch + 1) * 128],
                        carryb[cch % 2][:, :],
                        start=True, stop=True).inc(SEM_PE)
                    yo_pe = r.n[SEM_PE]
                    yo_pe_of = (yo_pe, yob)
                    r.v.wait(SEM_PE, yo_pe)
                    r.v.scalar_tensor_tensor(
                        cout[:, :], cin[:, :],
                        dab[:64, cch * 32 + h:cch * 32 + h + 1],
                        c.pb[ssb][:64, 0:64], ALU.mult, ALU.add).inc(SEM_VE)
                    r.v.wait(SEM_VE, r.n[SEM_VE])
                    r.v.tensor_copy(carryb[(cch + 1) % 2][:, :],
                                    cout[:, :]).inc(SEM_VE)
                    stt_prev = r.n[SEM_VE]
                    if cq == 0 and c4 == 0 and ysb_free is not None:
                        r.v.wait(*ysb_free)
                    if SSD_DEV == "yo" and h == 0:
                        r.v.tensor_copy(ydt[:, 0:64], c.pb[yob][:, 0:64]).inc(SEM_VE)
                        r.g.wait(SEM_VE, r.n[SEM_VE])
                        r.g.dma(c.yod[:, cch * 64:(cch + 1) * 64],
                                ydt[:, 0:64], SEM_ST)
                        r.g.dma(c.cbd[:, cch * 64:(cch + 1) * 64],
                                carryb[cch % 2][:, :], SEM_ST)
                        r.v.wait(SEM_ST, r.n[SEM_ST])
                    r.v.tensor_scalar_mul(
                        ysb[:, cch * 64:(cch + 1) * 64],
                        c.pb[yob][:, 0:64],
                        eacst[:, cch * 32 + h:cch * 32 + h + 1]).inc(SEM_VE)
                    bank_free[("b67", yob)] = (SEM_VE, r.n[SEM_VE])
                yd_pe_yo = r.n[SEM_PE]
                if ydt_free is not None:
                    r.v.wait(*ydt_free)
                r.v.wait(SEM_PE, yd_pe_yo)
                r.v.tensor_copy(ydt[:, :], c.pb[ydb][:, 0:256]).inc(SEM_VE)
                bank_free[("yd", ydb)] = (SEM_VE, r.n[SEM_VE])
                yv = ysb.rearrange("p (t j) -> p t j", j=64)[:, cq * 4:(cq + 1) * 4, :]
                r.v.wait(SEM_VE, r.n[SEM_VE])
                if SSD_DEV == "split":
                    if h == 0:
                        r.g.wait(SEM_VE, r.n[SEM_VE])
                        r.g.dma(c.ydd[:, cq * 256:(cq + 1) * 256],
                                ydt[:, :], SEM_ST)
                    r.v.nop().inc(SEM_VE)
                else:
                    r.v.tensor_add(yv, yv,
                                   ydt[:, :].rearrange("p (a b) -> p a b", a=4)
                                   ).inc(SEM_VE)
                bank_free[("yo",)] = (SEM_VE, r.n[SEM_VE])
                ydt_free = (SEM_VE, r.n[SEM_VE])
            bank_free[("xw",)] = (SEM_PE, r.n[SEM_PE])
            ysb_ve = r.n[SEM_VE]
            if h == 0:
                r.g.wait(SEM_VE, r.n[SEM_VE])
                r.g.dma(c.ysbd, ysb[:, :], SEM_ST)
                r.g.dma(c.ltd, LT[:, :], SEM_ST)
                r.g.dma(c.xtmd, xtm[:, :], SEM_ST)

            last_mt_ve = r.n[SEM_VE]
            if sq_free is not None:
                r.s.wait(*sq_free)
            r.s.wait(SEM_VE, ysb_ve)
            r.s.activation(sqb[:, :], ysb[:, :], AF.Square).inc(SEM_SC)
            r.v.wait(SEM_SC, r.n[SEM_SC])
            r.v.tensor_reduce(msb[:, 0:32],
                              sqb.rearrange("p (t j) -> p t j", j=64),
                              axis=AX.X, op=ALU.add).inc(SEM_VE)
            r.s.wait(SEM_VE, r.n[SEM_VE])
            r.s.activation(msb[:, 32:64], msb[:, 0:32], AF.Sqrt,
                           bias=c.consts[:, 2:3],
                           scale=1.0 / SP).inc(SEM_SC)
            r.v.wait(SEM_SC, r.n[SEM_SC])
            r.v.reciprocal(msb[:, 0:32], msb[:, 32:64]).inc(SEM_VE)
            r.v.wait(SEM_VE, r.n[SEM_VE])
            r.v.tensor_mul(ysb.rearrange("p (t j) -> p t j", j=64),
                           ysb.rearrange("p (t j) -> p t j", j=64),
                           msb[:, 0:32].unsqueeze(2)
                           .broadcast_to([128, 32, 64])).inc(SEM_VE)
            r.v.wait(SEM_VE, r.n[SEM_VE])
            r.v.tensor_mul(ysb.rearrange("p (t j) -> p t j", j=64),
                           ysb.rearrange("p (t j) -> p t j", j=64),
                           c.gnwb_sb[:, :].unsqueeze(1)
                           .broadcast_to([128, 32, 64])).inc(SEM_VE)
            yn_ve = r.n[SEM_VE]
            for cq in range(8):
                bk = (cq % 2)
                pv = c.pb[bk][:, :].bitcast(BF16)
                if bank_free.get(("gt", bk)) is not None:
                    r.t.wait(*bank_free[("gt", bk)])
                r.t.wait(SEM_VE, last_mt_ve)
                r.t.wait(SEM_LA, h_la)
                for c4 in range(4):
                    cch = cq * 4 + c4
                    ins = r.t.transpose(pv[:, c4 * 64:(c4 + 1) * 64],
                                        gf[:, cch * 128:(cch + 1) * 128],
                                        c.identb_sb[:64, 0:64])
                ins.inc(SEM_PE)
                if gtm_free is not None and cq == 0:
                    r.v.wait(*gtm_free)
                r.v.wait(SEM_PE, r.n[SEM_PE])
                r.v.tensor_copy(gtm[:, cq * 256:(cq + 1) * 256],
                                pv[:, 0:256]).inc(SEM_VE)
                bank_free[("gt", bk)] = (SEM_VE, r.n[SEM_VE])
            gf_free = (SEM_PE, r.n[SEM_PE])
            r.s.wait(SEM_VE, r.n[SEM_VE])
            if sq_free is not None:
                pass
            r.s.activation(sqb[:, :], gtm[:, :], AF.Silu).inc(SEM_SC)
            sq_free = (SEM_SC, r.n[SEM_SC])
            if yg_free is not None:
                r.v.wait(*yg_free)
            r.v.wait(SEM_SC, r.n[SEM_SC])
            r.v.wait(SEM_VE, yn_ve)
            r.v.tensor_mul(ygb[:, :], ysb[:, :], sqb[:, :]).inc(SEM_VE)
            ysb_free = (SEM_VE, r.n[SEM_VE])
            gtm_free = (SEM_VE, r.n[SEM_VE])
            if yst_free is not None:
                r.t.wait(*yst_free)
            for cq in range(8):
                bk = (cq % 2)
                pv = c.pb[bk][:, :].bitcast(BF16)
                if bank_free.get(("yt", bk)) is not None:
                    r.t.wait(*bank_free[("yt", bk)])
                r.t.wait(SEM_VE, r.n[SEM_VE])
                for c4 in range(4):
                    cch = cq * 4 + c4
                    ins = r.t.transpose(pv[:64, c4 * 128:(c4 + 1) * 128],
                                        ygb[:, cch * 64:(cch + 1) * 64],
                                        c.identb_sb)
                ins.inc(SEM_PE)
                r.v.wait(SEM_PE, r.n[SEM_PE])
                r.v.tensor_copy(yst[:64, cq * 512:(cq + 1) * 512],
                                pv[:64, 0:512]).inc(SEM_VE)
                bank_free[("yt", bk)] = (SEM_VE, r.n[SEM_VE])
            yg_free = (SEM_PE, r.n[SEM_PE])
            r.y.wait(SEM_VE, r.n[SEM_VE])
            jb, jhf = h // 2, h % 2
            r.y.dma(c.yfm[jb][jhf * 64:jhf * 64 + 64, :], yst[:64, :], SEM_ST)
            yst_free = (SEM_ST, r.n[SEM_ST])
            cf_free = (SEM_ST, r.n[SEM_ST])
        kv_free = (SEM_PE, r.n[SEM_PE])
    c.ssd_done = [(SEM_PE, r.n[SEM_PE]), (SEM_VE, r.n[SEM_VE]),
                  (SEM_ST, r.n[SEM_ST]), (SEM_SC, r.n[SEM_SC])]


def _p_debug(c):
    r = c.r
    for name, out in c.dbg.items():
        for sem in (SEM_VE, SEM_SC, SEM_PE, SEM_GP, SEM_AD, SEM_CC, SEM_ST):
            r.g.wait(sem, r.n[sem])
        if name == "LNX":
            r.g.dma(out, c.LNX[:, :], SEM_ST)
        else:
            src = getattr(c, name)
            if src.dtype == F32:
                r.g.dma(out, src, SEM_ST)
            else:
                r.g.dma(out, src, SEM_ST)


def _p_final(c):
    r = c.r
    r.y.wait(SEM_ST, r.n[SEM_ST])


def kernel(**inputs):
    nc = build_program()
    maps = make_in_maps(inputs)
    res = run_bass_kernel_spmd(nc, maps, core_ids=list(range(8)))
    out = np.stack([res.results[2 * b]["y"] for b in range(4)], axis=0)
    return out
